# version 04
# baseline (speedup 1.0000x reference)
import numpy as np
from contextlib import ExitStack
import concourse.bass as bass
import concourse.mybir as mybir
from concourse.bass_utils import run_bass_kernel_spmd

F32 = mybir.dt.float32
BF16 = mybir.dt.bfloat16
AF = mybir.ActivationFunctionType
ALU = mybir.AluOpType
AX = mybir.AxisListType

D = 2048
NDT = 16
MC = 1024
TC = 256
NCH = MC // TC
NST = 16
DTR = 128
FF = 4096
NF = 32
NE = 9
PADW = 3
XOS = PADW + MC
WTS = TC + PADW
NTT = 8
EPS = 1e-5

C_EPS = 0
C_NEGHALF = 1
C_NCONST = 4


class S:
    def __init__(self, stack, nc, name):
        self.h = stack.enter_context(nc.semaphore(name))
        self.v = 0


class Prog:

    ENGS = ("pe", "act", "dve", "pool", "sp")

    def __init__(self):
        self.q = {e: [] for e in self.ENGS}
        self.auto = {}

    def wait(self, eng, s, at=None):
        tgt = s.v if at is None else at
        if tgt > 0:
            self.q[eng].append(lambda E, h=s.h, t=tgt: E.wait_ge(h, t))

    def op(self, eng, fn, incs=(), noauto=False):
        a = self.auto.get(eng)
        if a is not None and not incs and not noauto:
            incs = ((a, 1),)
        if incs:
            def wrapped(E, f=fn, pairs=tuple((s.h, n) for s, n in incs)):
                ins = f(E)
                for h, n in pairs:
                    ins.then_inc(h, n)
                return ins
            self.q[eng].append(wrapped)
            for s, n in incs:
                s.v += n
        else:
            self.q[eng].append(fn)

    def fence(self, eng):
        self.wait(eng, self.auto[eng])


def build_program(debug=False, n_experts=NE):
    nc = bass.Bass("TRN2", target_bir_lowering=False, debug=False)
    dtr = nc.dram_tensor

    xdt_own = dtr("xdt_own", [D, XOS], F32, kind="ExternalInput").ap()
    xdt_pre = dtr("xdt_pre", [D, XOS], F32, kind="ExternalInput").ap()
    cvec = dtr("cvec", [128, C_NCONST], F32, kind="ExternalInput").ap()
    convw_t = dtr("convw_t", [128, NDT * 4], F32, kind="ExternalInput").ap()
    convb_t = dtr("convb_t", [128, NDT], F32, kind="ExternalInput").ap()
    dtprojb_t = dtr("dtprojb_t", [128, NDT], F32, kind="ExternalInput").ap()
    d1p_t = dtr("d1p_t", [128, NDT], F32, kind="ExternalInput").ap()
    alog_t = dtr("alog_t", [128, NDT * NST], F32, kind="ExternalInput").ap()
    xproj_t = dtr("xproj_t", [128, NDT * 160], F32, kind="ExternalInput").ap()
    dtproj_t = dtr("dtproj_t", [DTR, D], F32, kind="ExternalInput").ap()
    wg_t = dtr("wg_t", [128, NDT * 8], F32, kind="ExternalInput").ap()
    eye16 = dtr("eye16", [48, NST * 128], BF16, kind="ExternalInput").ap()
    ones1_128 = dtr("ones1_128", [1, 128], F32, kind="ExternalInput").ap()
    ones128_bf = dtr("ones128_bf", [128, 1], BF16, kind="ExternalInput").ap()
    ident_bf = dtr("ident_bf", [128, 128], BF16, kind="ExternalInput").ap()
    gate_w = dtr("gate_w", [NE * NF * 128, D], F32, kind="ExternalInput").ap()
    up_w = dtr("up_w", [NE * NF * 128, D], F32, kind="ExternalInput").ap()
    down_w = dtr("down_w", [NE * NF * 128, D], F32, kind="ExternalInput").ap()
    out_hbm = dtr("out", [MC, D], F32, kind="ExternalOutput").ap()
    if debug:
        dbg_xin = dtr("dbg_xin", [128, NDT * MC], F32, kind="ExternalOutput").ap()
        dbg_comb = dtr("dbg_comb", [128, NTT * 8], F32, kind="ExternalOutput").ap()

    P = Prog()

    with ExitStack() as stack:
        en = stack.enter_context

        def sbuf(name, shape, dtype, side="left"):
            return en(nc.sbuf_tensor(name, shape, dtype, side=side))

        sIN = S(stack, nc, "sIN")
        sRC = S(stack, nc, "sRC")
        sOUT = S(stack, nc, "sOUT")
        sW = S(stack, nc, "sW")
        sWg = [S(stack, nc, "sWg0"), S(stack, nc, "sWg1")]
        sWu = [S(stack, nc, "sWu0"), S(stack, nc, "sWu1")]
        sWd = [S(stack, nc, "sWd0"), S(stack, nc, "sWd1")]
        sPE = S(stack, nc, "sPE")
        sACT = S(stack, nc, "sACT")
        sDVE = S(stack, nc, "sDVE")
        sGf = S(stack, nc, "sGf")
        sUf = S(stack, nc, "sUf")
        sDf = S(stack, nc, "sDf")
        P.auto = {"dve": sDVE, "act": sACT}

        scv = sbuf("scv", [128, C_NCONST], F32)
        nc.const_aps.register(F32, EPS, scv[:, C_EPS:C_EPS + 1])
        nc.const_aps.register(F32, -0.5, scv[:, C_NEGHALF:C_NEGHALF + 1])
        swg = sbuf("swg", [128, NDT * 8], F32)
        sident = sbuf("sident", [128, 128], BF16)
        xb = sbuf("xb", [128, NDT * MC], BF16)
        carry = sbuf("carry", [128, NDT * NST], F32)
        comb = sbuf("comb", [128, NTT * 8], F32)
        lsc = sbuf("lsc", [128, 80], F32)

        barc = {"pe": 0, "act": 0, "dve": 0}

        def bwait(eng):
            P.wait(eng, sPE, barc["pe"])
            P.wait(eng, sACT, barc["act"])
            P.wait(eng, sDVE, barc["dve"])

        with ExitStack() as xo_stack:
            xo = xo_stack.enter_context(
                nc.sbuf_tensor("xo", [128, NDT * XOS], F32, side="right"))
            with ExitStack() as mstack:
                men = mstack.enter_context

                def msbuf(name, shape, dtype):
                    return men(nc.sbuf_tensor(name, shape, dtype, side="right"))

                sconvw = msbuf("sconvw", [128, NDT * 4], F32)
                sconvb = msbuf("sconvb", [128, NDT], F32)
                sdtpb = msbuf("sdtpb", [128, NDT], F32)
                sd1p = msbuf("sd1p", [128, NDT], F32)
                sA = msbuf("sA", [128, NDT * NST], F32)
                sxproj = msbuf("sxproj", [128, NDT * 160], BF16)
                sdtproj = msbuf("sdtproj", [DTR, D], BF16)
                seye = msbuf("seye", [48, NST * 128], BF16)
                sones1 = msbuf("sones1", [1, 128], F32)
                sonesb = msbuf("sonesb", [128, 1], BF16)
                sdelta = msbuf("sdelta", [128, TC], F32)
                sdx = msbuf("sdx", [128, TC], F32)
                ssp = msbuf("ssp", [128, 2 * TC], F32)
                srbq = msbuf("srbq", [128, 2 * TC], BF16)
                sst = msbuf("sst", [1, 3 * TC], F32)
                sdtin = msbuf("sdtin", [128, TC], BF16)
                sbc = msbuf("sbc", [64, TC], BF16)
                Bb = msbuf("Bb", [128, NST * TC], BF16)
                Cb = msbuf("Cb", [128, NST * TC], BF16)
                ta = msbuf("ta", [128, NST * TC], F32)
                tb = msbuf("tb", [128, NST * TC], F32)
                xcv = msbuf("xcv", [128, NDT * TC], BF16)
                wbuf = msbuf("wbuf", [128, NDT * WTS], F32)

                p_proj = men(nc.psum_tensor("p_proj", [128, TC], F32))
                p_bc = men(nc.psum_tensor("p_bc", [64, TC], F32))
                p_dl0 = men(nc.psum_tensor("p_dl0", [128, TC], F32))
                p_dl1 = men(nc.psum_tensor("p_dl1", [128, TC], F32))
                p_bb0 = men(nc.psum_tensor("p_bb0", [128, TC], F32))
                p_bb1 = men(nc.psum_tensor("p_bb1", [128, TC], F32))
                p_st = men(nc.psum_tensor("p_st", [1, 2 * TC], F32))
                p_bcst = men(nc.psum_tensor("p_bcst", [128, 2 * TC], F32))

                for dst, src in [
                    (scv[:], cvec), (sconvw[:], convw_t), (sconvb[:], convb_t),
                    (sdtpb[:], dtprojb_t), (sd1p[:], d1p_t), (sA[:], alog_t),
                    (swg[:], wg_t), (seye[:], eye16), (sones1[:], ones1_128),
                    (sonesb[:], ones128_bf), (sident[:], ident_bf),
                ]:
                    P.op("sp", lambda E, d=dst, s=src: E.dma_start(out=d, in_=s),
                         incs=[(sIN, 16)])
                P.op("pool", lambda E: E.dma_start(out=sxproj[:], in_=xproj_t),
                     incs=[(sW, 16)])
                P.op("pool", lambda E: E.dma_start(out=sdtproj[:], in_=dtproj_t),
                     incs=[(sW, 16)])
                P.op("sp", lambda E: E.dma_start(
                    out=xo[:].rearrange("p (dt t) -> p dt t", dt=NDT),
                    in_=xdt_own.rearrange("(dt p) t -> p dt t", p=128)),
                    incs=[(sIN, 16)])
                setup_in = sIN.v
                setup_w = sW.v

                P.wait("act", sIN, setup_in)
                P.op("act", lambda E: E.activation(sA[:], sA[:], AF.Exp))
                P.wait("dve", sACT)
                P.op("dve", lambda E: E.tensor_scalar_mul(sA[:], sA[:], -1.0))
                P.op("dve", lambda E: E.memset(carry[:], 0.0))
                amaterial = sDVE.v

                def mamba_chunk(rc, k):
                    base_t = k * TC
                    bwait("dve")
                    bwait("act")
                    bwait("pe")
                    if rc:
                        P.wait("dve", sRC)
                        P.wait("dve", sDVE, amaterial)
                    else:
                        P.wait("dve", sIN, setup_in)
                        for dti in range(NDT):
                            xoff = dti * XOS + PADW + base_t
                            P.op("dve", lambda E, o=xoff, d=dti: E.tensor_copy(
                                wbuf[:, d * WTS + PADW:d * WTS + PADW + TC],
                                xo[:, o:o + TC]))
                        P.fence("dve")
                    for dti in range(NDT):
                        wb = dti * WTS + PADW
                        xc = ssp[:, 0:TC]
                        if dti > 0:
                            P.wait("dve", sACT)
                        P.op("dve", lambda E, b=wb, d=dti: E.tensor_scalar_mul(
                            xc, wbuf[:, b - 3:b - 3 + TC],
                            sconvw[:, d * 4:d * 4 + 1]))
                        for j in range(1, 4):
                            P.fence("dve")
                            P.op("dve", lambda E, b=wb, d=dti, jj=j:
                                 E.scalar_tensor_tensor(
                                     xc, wbuf[:, b - 3 + jj:b - 3 + jj + TC],
                                     sconvw[:, d * 4 + jj:d * 4 + jj + 1], xc,
                                     ALU.mult, ALU.add))
                        P.wait("act", sDVE)
                        P.op("act", lambda E, d=dti: E.activation(
                            xcv[:, d * TC:(d + 1) * TC], xc, AF.Silu,
                            bias=sconvb[:, d:d + 1]))
                    xcv_done = sACT.v
                    P.wait("pe", sACT, xcv_done)
                    P.wait("pe", sW, setup_w)
                    for dti in range(NDT):
                        P.op("pe", lambda E, d=dti: E.matmul(
                            p_proj[:], sxproj[:, d * 160:d * 160 + 128],
                            xcv[:, d * TC:(d + 1) * TC],
                            start=(d == 0), stop=(d == NDT - 1)),
                            incs=[(sPE, 1)] if dti == NDT - 1 else ())
                    for dti in range(NDT):
                        P.op("pe", lambda E, d=dti: E.matmul(
                            p_bc[0:16, :], sxproj[:, d * 160 + 128:d * 160 + 144],
                            xcv[:, d * TC:(d + 1) * TC],
                            start=(d == 0), stop=(d == NDT - 1)),
                            incs=[(sPE, 1)] if dti == NDT - 1 else ())
                    if not rc:
                        for dti in range(NDT):
                            P.op("pe", lambda E, d=dti: E.matmul(
                                p_bc[32:48, :],
                                sxproj[:, d * 160 + 144:d * 160 + 160],
                                xcv[:, d * TC:(d + 1) * TC],
                                start=(d == 0), stop=(d == NDT - 1)),
                                incs=[(sPE, 1)] if dti == NDT - 1 else ())
                    P.wait("act", sPE)
                    P.op("act", lambda E: E.copy(sdtin[:], p_proj[:]))
                    P.op("act", lambda E: E.copy(sbc[0:16, :], p_bc[0:16, :]))
                    if not rc:
                        P.op("act", lambda E: E.copy(sbc[32:48, :], p_bc[32:48, :]))
                    bc_done = sACT.v
                    P.wait("pe", sACT, bc_done)
                    sel = [(n, 0, Bb) for n in range(NST)]
                    if not rc:
                        sel = [x for n in range(NST)
                               for x in ((n, 0, Bb), (n, 32, Cb))]
                    copy_dones = []
                    for i, (n, row, dstT) in enumerate(sel):
                        pb = p_bb0 if i % 2 == 0 else p_bb1
                        if i >= 2:
                            P.wait("pe", sDVE, copy_dones[i - 2])
                        P.op("pe", lambda E, nn=n, r=row, pp=pb: E.matmul(
                            pp[:], seye[r:r + 16, nn * 128:(nn + 1) * 128],
                            sbc[r:r + 16, :],
                            start=True, stop=True), incs=[(sPE, 1)])
                        P.wait("dve", sPE)
                        P.op("dve", lambda E, nn=n, pp=pb, dd=dstT: E.tensor_copy(
                            dd[:, nn * TC:(nn + 1) * TC], pp[:]))
                        copy_dones.append(sDVE.v)
                    for dti in range(NDT):
                        pdl = p_dl0 if dti % 2 == 0 else p_dl1
                        P.wait("pe", sACT)
                        P.op("pe", lambda E, d=dti, pp=pdl: E.matmul(
                            pp[:], sdtproj[:, d * 128:(d + 1) * 128], sdtin[:],
                            start=True, stop=True), incs=[(sPE, 1)])
                        P.wait("act", sPE)
                        P.op("act", lambda E, d=dti, pp=pdl: E.activation(
                            ssp[:, 0:TC], pp[:], AF.Abs, bias=sdtpb[:, d:d + 1]))
                        P.fence("act")
                        P.op("act", lambda E: E.activation(
                            ssp[:, 0:TC], ssp[:, 0:TC], AF.Exp, scale=-1.0))
                        P.fence("act")
                        P.op("act", lambda E: E.activation(
                            ssp[:, TC:2 * TC], ssp[:, 0:TC], AF.Ln, bias=1.0))
                        P.fence("act")
                        P.op("act", lambda E, d=dti, pp=pdl: E.activation(
                            ssp[:, 0:TC], pp[:], AF.Relu, bias=sdtpb[:, d:d + 1]))
                        P.wait("dve", sACT)
                        P.op("dve", lambda E: E.tensor_add(
                            sdelta[:], ssp[:, 0:TC], ssp[:, TC:2 * TC]))
                        delta_done = sDVE.v
                        P.wait("act", sDVE, delta_done)
                        for n in range(NST):
                            P.op("act", lambda E, d=dti, nn=n: E.activation(
                                ta[:, nn * TC:(nn + 1) * TC], sdelta[:], AF.Exp,
                                scale=sA[:, d * NST + nn:d * NST + nn + 1]))
                        a_done = sACT.v
                        P.fence("dve")
                        P.op("dve", lambda E, d=dti: E.tensor_mul(
                            sdx[:], sdelta[:], xcv[:, d * TC:(d + 1) * TC]))
                        P.fence("dve")
                        for n in range(NST):
                            P.op("dve", lambda E, nn=n: E.tensor_mul(
                                tb[:, nn * TC:(nn + 1) * TC], sdx[:],
                                Bb[:, nn * TC:(nn + 1) * TC]))
                        P.fence("dve")
                        P.wait("dve", sACT, a_done)
                        for n in range(NST):
                            P.op("dve", lambda E, d=dti, nn=n: E.tensor_tensor_scan(
                                tb[:, nn * TC:(nn + 1) * TC],
                                ta[:, nn * TC:(nn + 1) * TC],
                                tb[:, nn * TC:(nn + 1) * TC],
                                carry[:, d * NST + nn:d * NST + nn + 1],
                                ALU.mult, ALU.add))
                        P.fence("dve")
                        P.op("dve", lambda E, d=dti: E.tensor_copy(
                            carry[:, d * NST:(d + 1) * NST],
                            tb[:].rearrange("p (n t) -> p n t", n=NST)[:, :, TC - 1]))
                        if rc:
                            continue
                        P.op("dve", lambda E: E.tensor_mul(ta[:], tb[:], Cb[:]))
                        for w in (8 * TC, 4 * TC, 2 * TC, TC):
                            P.fence("dve")
                            P.op("dve", lambda E, ww=w: E.tensor_add(
                                ta[:, 0:ww], ta[:, 0:ww], ta[:, ww:2 * ww]))
                        xoff = dti * XOS + PADW + base_t
                        P.op("dve", lambda E, o=xoff, d=dti: E.tensor_copy(
                            wbuf[:, d * WTS:d * WTS + PADW],
                            xo[:, o + TC - PADW:o + TC]))
                        P.fence("dve")
                        P.op("dve", lambda E, o=xoff, d=dti: E.scalar_tensor_tensor(
                            xo[:, o:o + TC], xo[:, o:o + TC], sd1p[:, d:d + 1],
                            ta[:, 0:TC], ALU.mult, ALU.add))
                        if dti > 0:
                            P.wait("dve", sPE)
                        P.fence("dve")
                        P.op("dve", lambda E, o=xoff: E.tensor_copy(
                            srbq[:, 0:TC], xo[:, o:o + TC]))
                        P.wait("act", sDVE)
                        P.op("act", lambda E, o=xoff: E.activation(
                            srbq[:, TC:2 * TC], xo[:, o:o + TC], AF.Square))
                        P.wait("pe", sDVE)
                        P.wait("pe", sACT)
                        P.op("pe", lambda E, d=dti: E.matmul(
                            p_st[0:1, :], sonesb[:], srbq[:],
                            start=(d == 0), stop=(d == NDT - 1)),
                            incs=[(sPE, 1)])
                    if not rc:
                        P.wait("dve", sPE)
                        P.op("dve", lambda E: E.tensor_scalar_mul(
                            sst[0:1, 0:TC], p_st[0:1, 0:TC], 1.0 / D))
                        P.op("dve", lambda E: E.tensor_scalar_mul(
                            sst[0:1, TC:2 * TC], p_st[0:1, TC:2 * TC], 1.0 / D))
                        P.fence("dve")
                        P.op("dve", lambda E: E.tensor_mul(
                            sst[0:1, 2 * TC:3 * TC], sst[0:1, 0:TC],
                            sst[0:1, 0:TC]))
                        P.fence("dve")
                        P.op("dve", lambda E: E.tensor_sub(
                            sst[0:1, TC:2 * TC], sst[0:1, TC:2 * TC],
                            sst[0:1, 2 * TC:3 * TC]))
                        P.wait("act", sDVE)
                        P.op("act", lambda E: E.activation(
                            sst[0:1, 2 * TC:3 * TC], sst[0:1, TC:2 * TC], AF.Ln,
                            bias=scv[0:1, C_EPS:C_EPS + 1]))
                        P.fence("act")
                        P.op("act", lambda E: E.activation(
                            sst[0:1, TC:2 * TC], sst[0:1, 2 * TC:3 * TC], AF.Exp,
                            scale=scv[0:1, C_NEGHALF:C_NEGHALF + 1]))
                        P.wait("pe", sACT)
                        P.op("pe", lambda E: E.matmul(
                            p_bcst[:, 0:TC], sones1[:], sst[0:1, 0:TC],
                            start=True, stop=True))
                        P.op("pe", lambda E: E.matmul(
                            p_bcst[:, TC:2 * TC], sones1[:], sst[0:1, TC:2 * TC],
                            start=True, stop=True), incs=[(sPE, 1)])
                        P.wait("dve", sPE)
                        for dti in range(NDT):
                            xoff = dti * XOS + PADW + base_t
                            P.op("dve", lambda E, o=xoff: E.tensor_sub(
                                xo[:, o:o + TC], xo[:, o:o + TC], p_bcst[:, 0:TC]))
                            P.fence("dve")
                            P.op("dve", lambda E, o=xoff: E.tensor_mul(
                                xo[:, o:o + TC], xo[:, o:o + TC],
                                p_bcst[:, TC:2 * TC]))
                            P.fence("dve")
                            P.op("dve", lambda E, o=xoff, d=dti: E.tensor_copy(
                                xb[:, d * MC + base_t:d * MC + base_t + TC],
                                xo[:, o:o + TC]))
                    barc["pe"] = sPE.v
                    barc["act"] = sACT.v
                    barc["dve"] = sDVE.v

                for kk in range(NCH):
                    bwait("sp")
                    P.op("sp", lambda E, k=kk: E.dma_start(
                        out=wbuf[:].rearrange("p (dt t) -> p dt t", dt=NDT),
                        in_=xdt_pre.rearrange("(dt p) t -> p dt t", p=128)
                        [:, :, k * TC:k * TC + WTS]), incs=[(sRC, 16)])
                    mamba_chunk(True, kk)
                bwait("dve")
                P.wait("dve", sIN, setup_in)
                for dti in range(NDT):
                    P.op("dve", lambda E, d=dti: E.tensor_copy(
                        wbuf[:, d * WTS:d * WTS + PADW],
                        xo[:, d * XOS:d * XOS + PADW]))
                barc["dve"] = sDVE.v
                for kk in range(NCH):
                    mamba_chunk(False, kk)

                if debug:
                    bwait("sp")
                    for dti in range(NDT):
                        P.op("sp", lambda E, d=dti: E.dma_start(
                            out=dbg_xin[:, d * MC:(d + 1) * MC],
                            in_=xo[:, d * XOS + PADW:d * XOS + PADW + MC]),
                            incs=[(sOUT, 16)])

            acc = sbuf("acc", [128, NTT * D], F32)
            xloq = sbuf("xloq", [128, 4 * 128], BF16)
            sbt = sbuf("sbt", [128, 512], BF16)
            with ExitStack() as tstack:
                p_lg = tstack.enter_context(nc.psum_tensor("p_lg", [128, 8], F32))
                p_tr = tstack.enter_context(
                    nc.psum_tensor("p_tr", [128, 1024], BF16))

                bwait("pe")
                bwait("act")
                bwait("dve")
                for tt in range(NTT):
                    for dti in range(NDT):
                        P.op("pe", lambda E, d=dti, t=tt: E.matmul(
                            p_lg[:],
                            xo[:, d * XOS + PADW + t * 128:
                               d * XOS + PADW + (t + 1) * 128],
                            swg[:, d * 8:(d + 1) * 8],
                            start=(d == 0), stop=(d == NDT - 1)),
                            incs=[(sPE, 1)] if dti == NDT - 1 else ())

                    def r0(c):
                        return lsc[:, c * 8:(c + 1) * 8]
                    P.wait("dve", sPE)
                    P.op("dve", lambda E: E.reduce_max(lsc[:, 64:65], p_lg[:], AX.X))
                    P.fence("dve")
                    P.op("dve", lambda E: E.tensor_scalar_mul(
                        lsc[:, 65:66], lsc[:, 64:65], -1.0))
                    P.op("dve", lambda E: E.tensor_copy(r0(0), p_lg[:]))
                    P.wait("act", sDVE)
                    P.op("act", lambda E: E.activation(
                        r0(1), r0(0), AF.Exp, bias=lsc[:, 65:66]))
                    P.wait("dve", sACT)
                    P.op("dve", lambda E: E.reduce_max(lsc[:, 64:65], r0(1), AX.X))
                    P.fence("dve")
                    P.op("dve", lambda E: E.tensor_single_scalar(
                        r0(2), r0(1), lsc[:, 64:65], ALU.is_equal))
                    P.fence("dve")
                    P.op("dve", lambda E: E.tensor_scalar(
                        r0(3), r0(2), -1.0, 1.0, ALU.mult, ALU.add))
                    P.fence("dve")
                    P.op("dve", lambda E: E.tensor_mul(r0(3), r0(3), r0(1)))
                    P.fence("dve")
                    P.op("dve", lambda E: E.reduce_max(lsc[:, 66:67], r0(3), AX.X))
                    P.fence("dve")
                    P.op("dve", lambda E: E.tensor_single_scalar(
                        r0(4), r0(3), lsc[:, 66:67], ALU.is_equal))
                    P.fence("dve")
                    P.op("dve", lambda E: E.tensor_add(r0(2), r0(2), r0(4)))
                    P.op("dve", lambda E: E.tensor_add(
                        lsc[:, 67:68], lsc[:, 64:65], lsc[:, 66:67]))
                    P.fence("dve")
                    P.op("dve", lambda E: E.reciprocal(lsc[:, 68:69], lsc[:, 67:68]))
                    P.fence("dve")
                    P.op("dve", lambda E: E.tensor_mul(r0(2), r0(2), r0(1)))
                    P.fence("dve")
                    P.op("dve", lambda E, t=tt: E.tensor_scalar_mul(
                        comb[:, t * 8:(t + 1) * 8], r0(2), lsc[:, 68:69]))
                    P.wait("pe", sDVE)

                if debug:
                    P.wait("sp", sDVE)
                    P.op("sp", lambda E: E.dma_start(out=dbg_comb, in_=comb[:]),
                         incs=[(sOUT, 16)])

                for tt in range(NTT):
                    for dq in range(4):
                        P.wait("dve", sPE)
                        for i in range(4):
                            dti = dq * 4 + i
                            xoff = dti * XOS + PADW + tt * 128
                            P.op("dve", lambda E, o=xoff, d=dti, ii=i, t=tt:
                                 E.tensor_sub(
                                     xloq[:, ii * 128:(ii + 1) * 128],
                                     xo[:, o:o + 128],
                                     xb[:, d * MC + t * 128:d * MC + t * 128 + 128]))
                        P.wait("pe", sDVE)
                        for i in range(4):
                            dti = dq * 4 + i
                            P.op("pe", lambda E, d=dti, ii=i, t=tt: E.transpose(
                                p_tr[:, ii * 128:(ii + 1) * 128],
                                xb[:, d * MC + t * 128:d * MC + t * 128 + 128],
                                sident[:]))
                            P.op("pe", lambda E, ii=i: E.transpose(
                                p_tr[:, 512 + ii * 128:512 + (ii + 1) * 128],
                                xloq[:, ii * 128:(ii + 1) * 128], sident[:]),
                                incs=[(sPE, 1)] if i == 3 else ())
                        P.wait("dve", sPE)
                        P.op("dve", lambda E: E.tensor_copy(sbt[:], p_tr[:, 0:512]))
                        P.fence("dve")
                        P.op("dve", lambda E, t=tt, q=dq: E.tensor_add(
                            acc[:, t * D + q * 512:t * D + (q + 1) * 512],
                            sbt[:], p_tr[:, 512:1024]))
                seed_dve = sDVE.v

        H = sbuf("H", [128, NF * MC], BF16, side="right")
        wgb = [sbuf("wgb0", [128, D], BF16), sbuf("wgb1", [128, D], BF16)]
        wub = [sbuf("wub0", [128, D], BF16), sbuf("wub1", [128, D], BF16)]
        wdb = [sbuf("wdb0", [128, D], BF16), sbuf("wdb1", [128, D], BF16)]
        tsil = sbuf("tsil", [128, MC], F32)
        sq16 = sbuf("sq16", [128, D], BF16)
        pG = en(nc.psum_tensor("pG", [128, MC], F32))
        pU = en(nc.psum_tensor("pU", [128, MC], F32))
        pdA = en(nc.psum_tensor("pdA", [128, MC], F32))
        pdB = en(nc.psum_tensor("pdB", [128, MC], F32))

        def pd_slice(blk, tt):
            grpA = (blk % 2 == 0)
            if tt < 4:
                base = pG if grpA else pdA
            else:
                base = pU if grpA else pdB
            c = (tt % 4) * 256
            return base[:, c:c + 256]

        ngl = [0]
        ndl = [0]
        silu_cap = []
        hm_cap = []
        ev_cap = []

        for e in range(n_experts):
            for f in range(NF):
                fgi = e * NF + f
                par = f % 2
                if ngl[0] == 0:
                    P.wait("pool", sDVE, seed_dve)
                if ngl[0] >= 2:
                    P.wait("pool", sGf, ngl[0] - 1)
                    P.wait("pool", sUf, ngl[0] - 1)
                row = (e * NF + f) * 128
                P.op("pool", lambda E, r=row, p=par: E.dma_start(
                    out=wgb[p][:], in_=gate_w[r:r + 128, :]),
                    incs=[(sWg[par], 16)])
                P.op("pool", lambda E, r=row, p=par: E.dma_start(
                    out=wub[p][:], in_=up_w[r:r + 128, :]),
                    incs=[(sWu[par], 16)])
                ngl[0] += 1
                P.wait("pe", sWg[par], sWg[par].v)
                if fgi > 0:
                    P.wait("pe", sACT, silu_cap[fgi - 1])
                if e == 0 and f == 0:
                    P.wait("pe", sDVE, seed_dve)
                if f == 0 and e > 0:
                    P.wait("pe", sDVE, ev_cap[-1])
                for dti in range(NDT):
                    for h in range(2):
                        P.op("pe", lambda E, d=dti, hh=h, p=par: E.matmul(
                            pG[:, hh * 512:(hh + 1) * 512],
                            wgb[p][:, d * 128:(d + 1) * 128],
                            xb[:, d * MC + hh * 512:d * MC + (hh + 1) * 512],
                            start=(d == 0), stop=(d == NDT - 1)),
                            incs=[(sGf, 1)] if (dti == NDT - 1 and h == 1) else ())
                P.wait("pe", sWu[par], sWu[par].v)
                if fgi > 0:
                    P.wait("pe", sDVE, hm_cap[fgi - 1])
                for dti in range(NDT):
                    for h in range(2):
                        P.op("pe", lambda E, d=dti, hh=h, p=par: E.matmul(
                            pU[:, hh * 512:(hh + 1) * 512],
                            wub[p][:, d * 128:(d + 1) * 128],
                            xb[:, d * MC + hh * 512:d * MC + (hh + 1) * 512],
                            start=(d == 0), stop=(d == NDT - 1)),
                            incs=[(sUf, 1)] if (dti == NDT - 1 and h == 1) else ())
                P.wait("act", sGf, fgi + 1)
                if fgi > 0:
                    P.wait("act", sDVE, hm_cap[fgi - 1])
                P.op("act", lambda E: E.activation(tsil[:], pG[:], AF.Silu))
                silu_cap.append(sACT.v)
                P.wait("dve", sACT, silu_cap[fgi])
                P.wait("dve", sUf, sUf.v)
                P.op("dve", lambda E, ff=f: E.tensor_tensor(
                    H[:, ff * MC:(ff + 1) * MC], tsil[:], pU[:], ALU.mult))
                hm_cap.append(sDVE.v)
            P.wait("pe", sDVE, hm_cap[-1])
            for blk in range(8):
                for fq in range(4):
                    if ndl[0] >= 2:
                        P.wait("pool", sDf, ndl[0] - 1)
                    rowd = (e * NF + blk * 4 + fq) * 128
                    pard = ndl[0] % 2
                    P.op("pool", lambda E, r=rowd, p=pard: E.dma_start(
                        out=wdb[p][:], in_=down_w[r:r + 128, :]),
                        incs=[(sWd[pard], 16)])
                    ndl[0] += 1
                    P.wait("pe", sWd[pard], sWd[pard].v)
                    if fq == 0 and len(ev_cap) >= 2:
                        P.wait("pe", sDVE, ev_cap[-2])
                    for f8 in range(8):
                        f = fq * 8 + f8
                        for tt in range(NTT):
                            last_piece = (f8 == 7 and tt == NTT - 1)
                            P.op("pe", lambda E, ff=f, t=tt, b=blk, p=pard, q=fq,
                                 f8_=f8: E.matmul(
                                     pd_slice(b, t),
                                     H[:, ff * MC + t * 128:ff * MC + (t + 1) * 128],
                                     wdb[p][:, f8_ * 256:(f8_ + 1) * 256],
                                     start=(q == 0 and f8_ == 0 and t % 2 == 0),
                                     stop=(q == 3 and f8_ == 7)),
                                 incs=[(sDf, 1)] if last_piece else ())
                P.wait("dve", sDf, ndl[0])
                for tt in range(NTT):
                    P.fence("dve")
                    if e == 0:
                        P.op("dve", lambda E, b=blk, t=tt: E.scalar_tensor_tensor(
                            acc[:, t * D + b * 256:t * D + (b + 1) * 256],
                            pd_slice(b, t), 1.0,
                            acc[:, t * D + b * 256:t * D + (b + 1) * 256],
                            ALU.mult, ALU.add))
                    else:
                        P.op("dve", lambda E, b=blk, t=tt, ee=e:
                             E.scalar_tensor_tensor(
                                 acc[:, t * D + b * 256:t * D + (b + 1) * 256],
                                 pd_slice(b, t),
                                 comb[:, t * 8 + ee - 1:t * 8 + ee],
                                 acc[:, t * D + b * 256:t * D + (b + 1) * 256],
                                 ALU.mult, ALU.add))
                ev_cap.append(sDVE.v)

        P.wait("dve", sDVE, ev_cap[-1])
        for tt in range(NTT):
            a_tt = acc[:, tt * D:(tt + 1) * D]
            P.fence("dve")
            P.op("dve", lambda E, a=a_tt: E.reduce_sum(lsc[:, 64:65], a, AX.X))
            P.fence("dve")
            P.op("dve", lambda E: E.tensor_scalar_mul(
                lsc[:, 64:65], lsc[:, 64:65], 1.0 / D))
            P.fence("dve")
            P.op("dve", lambda E, a=a_tt: E.tensor_single_scalar(
                a, a, lsc[:, 64:65], ALU.subtract))
            P.wait("act", sDVE)
            P.op("act", lambda E, a=a_tt: E.activation(sq16[:], a, AF.Square))
            P.wait("dve", sACT)
            P.op("dve", lambda E: E.reduce_sum(lsc[:, 65:66], sq16[:], AX.X))
            P.fence("dve")
            P.op("dve", lambda E: E.tensor_scalar_mul(
                lsc[:, 65:66], lsc[:, 65:66], 1.0 / D))
            P.wait("act", sDVE)
            P.op("act", lambda E: E.activation(
                lsc[:, 66:67], lsc[:, 65:66], AF.Ln,
                bias=scv[:, C_EPS:C_EPS + 1]))
            P.fence("act")
            P.op("act", lambda E: E.activation(
                lsc[:, 67:68], lsc[:, 66:67], AF.Exp,
                scale=scv[:, C_NEGHALF:C_NEGHALF + 1]))
            P.wait("dve", sACT)
            P.op("dve", lambda E, a=a_tt: E.tensor_scalar_mul(
                a, a, lsc[:, 67:68]))
            P.wait("sp", sDVE)
            P.op("sp", lambda E, t=tt, a=a_tt: E.dma_start(
                out=out_hbm[t * 128:(t + 1) * 128, :], in_=a),
                incs=[(sOUT, 16)])
        fin = sOUT.v
        P.q["sp"].append(lambda E, t=fin: E.wait_ge(sOUT.h, t))

        with nc.Block() as block:
            @block.tensor
            def _(E):
                for fn in P.q["pe"]:
                    fn(E)

            @block.scalar
            def _(E):
                for fn in P.q["act"]:
                    fn(E)

            @block.vector
            def _(E):
                for fn in P.q["dve"]:
                    fn(E)

            @block.gpsimd
            def _(E):
                for fn in P.q["pool"]:
                    fn(E)

            @block.sync
            def _(E):
                for fn in P.q["sp"]:
                    fn(E)

    return nc



def _prep_shared(i):
    f32 = np.float32
    import ml_dtypes
    sh = {}
    sh["cvec"] = np.tile(np.array([[EPS, -0.5, 0.0, 0.0]], dtype=f32), (128, 1))
    cw = np.ascontiguousarray(np.asarray(i["L0_attn_conv_w"], f32)[:, 0, :])
    sh["convw_t"] = np.ascontiguousarray(
        cw.reshape(NDT, 128, 4).transpose(1, 0, 2).reshape(128, NDT * 4))
    for name, vec in [("convb_t", i["L0_attn_conv_b"]),
                      ("dtprojb_t", i["L0_attn_dtproj_b"]),
                      ("d1p_t", 1.0 + np.asarray(i["L0_attn_D_param"], f32))]:
        sh[name] = np.ascontiguousarray(
            np.asarray(vec, dtype=f32).reshape(NDT, 128).T)
    sh["alog_t"] = np.ascontiguousarray(
        np.asarray(i["L0_attn_A_log"], f32).reshape(NDT, 128, NST)
        .transpose(1, 0, 2).reshape(128, NDT * NST))
    xp = np.asarray(i["L0_attn_xproj_w"], f32)
    sh["xproj_t"] = np.ascontiguousarray(
        xp.T.reshape(NDT, 128, 160).transpose(1, 0, 2).reshape(128, NDT * 160))
    sh["dtproj_t"] = np.ascontiguousarray(np.asarray(i["L0_attn_dtproj_w"], f32).T)
    wg = np.asarray(i["L0_ffn_wg"], f32)
    sh["wg_t"] = np.ascontiguousarray(
        wg.T.reshape(NDT, 128, 8).transpose(1, 0, 2).reshape(128, NDT * 8))
    eye = np.zeros((48, NST * 128), dtype=f32)
    for n in range(NST):
        eye[n, n * 128:(n + 1) * 128] = 1.0
        eye[32 + n, n * 128:(n + 1) * 128] = 1.0
    sh["eye16"] = eye.astype(ml_dtypes.bfloat16)
    sh["ones1_128"] = np.ones((1, 128), dtype=f32)
    sh["ones128_bf"] = np.ones((128, 1), dtype=ml_dtypes.bfloat16)
    sh["ident_bf"] = np.eye(128, dtype=f32).astype(ml_dtypes.bfloat16)

    def tile_gu(mats):
        out = np.empty((NE * NF * 128, D), dtype=f32)
        for e, W in enumerate(mats):
            t = W.reshape(NF, 128, NDT, 128).transpose(0, 3, 2, 1)
            out[e * NF * 128:(e + 1) * NF * 128] = t.reshape(NF * 128, D)
        return out

    gmats = [np.asarray(i["L0_ffn_sh_gate"], f32)] + \
            [np.asarray(i["L0_ffn_ex_gate"][e], f32) for e in range(8)]
    umats = [np.asarray(i["L0_ffn_sh_up"], f32)] + \
            [np.asarray(i["L0_ffn_ex_up"][e], f32) for e in range(8)]
    sh["gate_w"] = tile_gu(gmats)
    sh["up_w"] = tile_gu(umats)

    def tile_down(mats):
        out = np.empty((NE * NF * 128, D), dtype=f32)
        for e, Wd in enumerate(mats):
            t = Wd.reshape(8, 256, 4, 8, 128)
            t = t.transpose(0, 2, 4, 3, 1)
            out[e * NF * 128:(e + 1) * NF * 128] = t.reshape(NF * 128, D)
        return out

    dmats = [np.asarray(i["L0_ffn_sh_down"], f32)] + \
            [np.asarray(i["L0_ffn_ex_down"][e], f32) for e in range(8)]
    sh["down_w"] = tile_down(dmats)
    return sh


def _prep_core(x, b, h):
    f32 = np.float32
    xt = np.asarray(x[b], f32).T
    own = np.zeros((D, XOS), dtype=f32)
    own[:, PADW:] = xt[:, h * MC:(h + 1) * MC]
    if h == 1:
        own[:, 0:PADW] = xt[:, MC - PADW:MC]
    pre = np.zeros((D, XOS), dtype=f32)
    if h == 1:
        pre[:, PADW:] = xt[:, 0:MC]
    return np.ascontiguousarray(own), np.ascontiguousarray(pre)


def kernel(**inputs):
    x = np.asarray(inputs["L0_attn_x"], np.float32)
    sh = _prep_shared(inputs)
    nc = build_program(debug=False)
    in_maps = []
    for core in range(8):
        b, h = core // 2, core % 2
        own, pre = _prep_core(x, b, h)
        m = dict(sh)
        m["xdt_own"] = own
        m["xdt_pre"] = pre
        in_maps.append(m)
    res = run_bass_kernel_spmd(nc, in_maps, core_ids=list(range(8)))
    out = np.empty((4, 2048, D), dtype=np.float32)
    for core in range(8):
        b, h = core // 2, core % 2
        out[b, h * MC:(h + 1) * MC, :] = res.results[core]["out"]
    return out


# version 05
# speedup vs baseline: 1.0247x; 1.0247x over previous
import numpy as np
from contextlib import ExitStack
import concourse.bass as bass
import concourse.mybir as mybir
from concourse.bass_utils import run_bass_kernel_spmd

F32 = mybir.dt.float32
BF16 = mybir.dt.bfloat16
AF = mybir.ActivationFunctionType
ALU = mybir.AluOpType
AX = mybir.AxisListType

D = 2048
NDT = 16
MC = 1024
TC = 256
NCH = MC // TC
NST = 16
DTR = 128
FF = 4096
NF = 32
NE = 9
PADW = 3
XOS = PADW + MC
WTS = TC + PADW
NTT = 8
EPS = 1e-5

C_EPS = 0
C_NEGHALF = 1
C_ZERO = 2
C_NCONST = 4
NH = NST // 2


class S:
    def __init__(self, stack, nc, name):
        self.h = stack.enter_context(nc.semaphore(name))
        self.v = 0


class Prog:

    ENGS = ("pe", "act", "dve", "pool", "sp")

    def __init__(self):
        self.q = {e: [] for e in self.ENGS}
        self.auto = {}

    def wait(self, eng, s, at=None):
        tgt = s.v if at is None else at
        if tgt > 0:
            self.q[eng].append(lambda E, h=s.h, t=tgt: E.wait_ge(h, t))

    def op(self, eng, fn, incs=(), noauto=False):
        a = self.auto.get(eng)
        if a is not None and not incs and not noauto:
            incs = ((a, 1),)
        if incs:
            def wrapped(E, f=fn, pairs=tuple((s.h, n) for s, n in incs)):
                ins = f(E)
                for h, n in pairs:
                    ins.then_inc(h, n)
                return ins
            self.q[eng].append(wrapped)
            for s, n in incs:
                s.v += n
        else:
            self.q[eng].append(fn)

    def fence(self, eng):
        self.wait(eng, self.auto[eng])


def build_program(debug=False, n_experts=NE):
    nc = bass.Bass("TRN2", target_bir_lowering=False, debug=False)
    dtr = nc.dram_tensor

    xdt_own = dtr("xdt_own", [D, XOS], F32, kind="ExternalInput").ap()
    xdt_pre = dtr("xdt_pre", [D, XOS], F32, kind="ExternalInput").ap()
    rcmask = dtr("rcmask", [128, 1], F32, kind="ExternalInput").ap()
    cvec = dtr("cvec", [128, C_NCONST], F32, kind="ExternalInput").ap()
    convw_t = dtr("convw_t", [128, NDT * 4], F32, kind="ExternalInput").ap()
    convb_t = dtr("convb_t", [128, NDT], F32, kind="ExternalInput").ap()
    dtprojb_t = dtr("dtprojb_t", [128, NDT], F32, kind="ExternalInput").ap()
    d1p_t = dtr("d1p_t", [128, NDT], F32, kind="ExternalInput").ap()
    alog_t = dtr("alog_t", [128, NDT * NST], F32, kind="ExternalInput").ap()
    xproj_t = dtr("xproj_t", [128, NDT * 160], F32, kind="ExternalInput").ap()
    dtproj_t = dtr("dtproj_t", [DTR, D], F32, kind="ExternalInput").ap()
    wg_t = dtr("wg_t", [128, NDT * 8], F32, kind="ExternalInput").ap()
    eye16 = dtr("eye16", [48, NST * 128], BF16, kind="ExternalInput").ap()
    ones1_128 = dtr("ones1_128", [1, 128], F32, kind="ExternalInput").ap()
    ones128_bf = dtr("ones128_bf", [128, 1], BF16, kind="ExternalInput").ap()
    ident_bf = dtr("ident_bf", [128, 128], BF16, kind="ExternalInput").ap()
    gate_w = dtr("gate_w", [NE * NF * 128, D], F32, kind="ExternalInput").ap()
    up_w = dtr("up_w", [NE * NF * 128, D], F32, kind="ExternalInput").ap()
    down_w = dtr("down_w", [NE * NF * 128, D], F32, kind="ExternalInput").ap()
    out_hbm = dtr("out", [MC, D], F32, kind="ExternalOutput").ap()
    if debug:
        dbg_xin = dtr("dbg_xin", [128, NDT * MC], F32, kind="ExternalOutput").ap()
        dbg_comb = dtr("dbg_comb", [128, NTT * 8], F32, kind="ExternalOutput").ap()

    P = Prog()

    with ExitStack() as stack:
        en = stack.enter_context

        def sbuf(name, shape, dtype, side="left"):
            return en(nc.sbuf_tensor(name, shape, dtype, side=side))

        sIN = S(stack, nc, "sIN")
        sRC = S(stack, nc, "sRC")
        sOUT = S(stack, nc, "sOUT")
        sW = S(stack, nc, "sW")
        sWg = [S(stack, nc, "sWg0"), S(stack, nc, "sWg1")]
        sWu = [S(stack, nc, "sWu0"), S(stack, nc, "sWu1")]
        sWd = [S(stack, nc, "sWd0"), S(stack, nc, "sWd1")]
        sPE = S(stack, nc, "sPE")
        sACT = S(stack, nc, "sACT")
        sDVE = S(stack, nc, "sDVE")
        sGf = S(stack, nc, "sGf")
        sUf = S(stack, nc, "sUf")
        sDf = S(stack, nc, "sDf")
        sPOOL = S(stack, nc, "sPOOL")
        P.auto = {"dve": sDVE, "act": sACT, "pool": sPOOL}

        scv = sbuf("scv", [128, C_NCONST], F32)
        nc.const_aps.register(F32, EPS, scv[:, C_EPS:C_EPS + 1])
        nc.const_aps.register(F32, -0.5, scv[:, C_NEGHALF:C_NEGHALF + 1])
        swg = sbuf("swg", [128, NDT * 8], F32)
        sident = sbuf("sident", [128, 128], BF16)
        xb = sbuf("xb", [128, NDT * MC], BF16)
        carry = sbuf("carry", [128, NDT * NST], F32)
        comb = sbuf("comb", [128, NTT * 8], F32)
        lsc = sbuf("lsc", [128, 80], F32)

        barc = {"pe": 0, "act": 0, "dve": 0, "pool": 0}

        def bwait(eng):
            P.wait(eng, sPE, barc["pe"])
            P.wait(eng, sACT, barc["act"])
            P.wait(eng, sDVE, barc["dve"])
            P.wait(eng, sPOOL, barc["pool"])

        with ExitStack() as xo_stack:
            xo = xo_stack.enter_context(
                nc.sbuf_tensor("xo", [128, NDT * XOS], F32, side="right"))
            with ExitStack() as mstack:
                men = mstack.enter_context

                def msbuf(name, shape, dtype):
                    return men(nc.sbuf_tensor(name, shape, dtype, side="right"))

                sconvw = msbuf("sconvw", [128, NDT * 4], F32)
                sconvb = msbuf("sconvb", [128, NDT], F32)
                sdtpb = msbuf("sdtpb", [128, NDT], F32)
                sd1p = msbuf("sd1p", [128, NDT], F32)
                sA = msbuf("sA", [128, NDT * NST], F32)
                sxproj = msbuf("sxproj", [128, NDT * 160], BF16)
                sdtproj = msbuf("sdtproj", [DTR, D], BF16)
                seye = msbuf("seye", [48, NST * 128], BF16)
                sones1 = msbuf("sones1", [1, 128], F32)
                sonesb = msbuf("sonesb", [128, 1], BF16)
                sdelta = msbuf("sdelta", [128, TC], F32)
                sdx = msbuf("sdx", [128, TC], F32)
                ssp = msbuf("ssp", [128, 2 * TC], F32)
                srbq = msbuf("srbq", [128, 2 * TC], BF16)
                sst = msbuf("sst", [1, 3 * TC], F32)
                sdtin = msbuf("sdtin", [128, TC], BF16)
                sbc = msbuf("sbc", [64, TC], BF16)
                srcm = msbuf("srcm", [128, 1], F32)
                sinj = msbuf("sinj", [128, NH], F32)
                hbuf = msbuf("hbuf", [128, NDT * PADW], F32)
                y0 = msbuf("y0", [128, TC], F32)
                Bb = msbuf("Bb", [128, NST * TC], BF16)
                Cb = msbuf("Cb", [128, NST * TC], BF16)
                tap = [msbuf("ta0", [128, NH * TC], F32),
                       msbuf("ta1", [128, NH * TC], F32)]
                tbp = [msbuf("tb0", [128, NH * TC], F32),
                       msbuf("tb1", [128, NH * TC], F32)]
                xcv = msbuf("xcv", [128, NDT * TC], BF16)
                wbufv = xb[:].bitcast(F32)

                p_proj = men(nc.psum_tensor("p_proj", [128, TC], F32))
                p_bc = men(nc.psum_tensor("p_bc", [64, TC], F32))
                p_dl0 = men(nc.psum_tensor("p_dl0", [128, TC], F32))
                p_dl1 = men(nc.psum_tensor("p_dl1", [128, TC], F32))
                p_bb0 = men(nc.psum_tensor("p_bb0", [128, TC], F32))
                p_bb1 = men(nc.psum_tensor("p_bb1", [128, TC], F32))
                p_st = men(nc.psum_tensor("p_st", [1, 2 * TC], F32))
                p_bcst = men(nc.psum_tensor("p_bcst", [128, 2 * TC], F32))

                for dst, src in [
                    (scv[:], cvec), (sconvw[:], convw_t), (sconvb[:], convb_t),
                    (sdtpb[:], dtprojb_t), (sd1p[:], d1p_t), (sA[:], alog_t),
                    (swg[:], wg_t), (seye[:], eye16), (sones1[:], ones1_128),
                    (sonesb[:], ones128_bf), (sident[:], ident_bf),
                    (srcm[:], rcmask),
                ]:
                    P.op("sp", lambda E, d=dst, s=src: E.dma_start(out=d, in_=s),
                         incs=[(sIN, 16)])
                P.op("pool", lambda E: E.dma_start(out=sxproj[:], in_=xproj_t),
                     incs=[(sW, 16)])
                P.op("pool", lambda E: E.dma_start(out=sdtproj[:], in_=dtproj_t),
                     incs=[(sW, 16)])
                P.op("sp", lambda E: E.dma_start(
                    out=xo[:].rearrange("p (dt t) -> p dt t", dt=NDT),
                    in_=xdt_own.rearrange("(dt p) t -> p dt t", p=128)),
                    incs=[(sIN, 16)])
                setup_in = sIN.v
                setup_w = sW.v

                P.wait("act", sIN, setup_in)
                P.op("act", lambda E: E.activation(sA[:], sA[:], AF.Exp))
                P.wait("dve", sACT)
                P.op("dve", lambda E: E.tensor_scalar_mul(sA[:], sA[:], -1.0))
                P.op("dve", lambda E: E.memset(carry[:], 0.0))
                amaterial = sDVE.v

                ta3 = [t[:].rearrange("p (n t) -> p n t", n=NH) for t in tap]
                tb3 = [t[:].rearrange("p (n t) -> p n t", n=NH) for t in tbp]
                ta_free = [0, 0]
                tb_free = [0, 0]
                srb_free = [0]
                ssp_free = [0]
                bc_copy_done = [0]

                def mamba_chunk(rc, k):
                    base_t = k * TC
                    bwait("dve")
                    bwait("act")
                    bwait("pe")
                    bwait("pool")
                    if rc:
                        P.wait("dve", sRC)
                        P.wait("dve", sDVE, amaterial)
                    else:
                        P.wait("dve", sIN, setup_in)
                    for dti in range(NDT):
                        xc = ssp[:, 0:TC]
                        if dti > 0:
                            P.wait("dve", sACT)
                        if rc or k == 0:
                            src = wbufv if rc else xo[:]
                            b0 = dti * (WTS if rc else XOS) + PADW
                            P.op("dve", lambda E, s=src, b=b0, d=dti:
                                 E.tensor_scalar_mul(
                                     xc, s[:, b - 3:b - 3 + TC],
                                     sconvw[:, d * 4:d * 4 + 1]))
                            for j in range(1, 4):
                                P.fence("dve")
                                P.op("dve", lambda E, s=src, b=b0, d=dti, jj=j:
                                     E.scalar_tensor_tensor(
                                         xc, s[:, b - 3 + jj:b - 3 + jj + TC],
                                         sconvw[:, d * 4 + jj:d * 4 + jj + 1], xc,
                                         ALU.mult, ALU.add))
                        else:
                            o = dti * XOS + PADW + base_t
                            P.op("dve", lambda E, oo=o, d=dti:
                                 E.tensor_scalar_mul(
                                     xc[:, 3:TC], xo[:, oo:oo + TC - 3],
                                     sconvw[:, d * 4:d * 4 + 1]))
                            P.op("dve", lambda E, d=dti: E.tensor_scalar_mul(
                                xc[:, 0:3], hbuf[:, d * 3:d * 3 + 3],
                                sconvw[:, d * 4:d * 4 + 1]))
                            for j in range(1, 4):
                                P.fence("dve")
                                P.op("dve", lambda E, oo=o, d=dti, jj=j:
                                     E.scalar_tensor_tensor(
                                         xc[:, 3 - jj:TC],
                                         xo[:, oo:oo + TC - 3 + jj],
                                         sconvw[:, d * 4 + jj:d * 4 + jj + 1],
                                         xc[:, 3 - jj:TC], ALU.mult, ALU.add))
                                if j < 3:
                                    P.op("dve", lambda E, d=dti, jj=j:
                                         E.scalar_tensor_tensor(
                                             xc[:, 0:3 - jj],
                                             hbuf[:, d * 3 + jj:d * 3 + 3],
                                             sconvw[:, d * 4 + jj:d * 4 + jj + 1],
                                             xc[:, 0:3 - jj], ALU.mult, ALU.add))
                        P.wait("act", sDVE)
                        P.op("act", lambda E, d=dti: E.activation(
                            xcv[:, d * TC:(d + 1) * TC], xc, AF.Silu,
                            bias=sconvb[:, d:d + 1]))
                    conv_dve_done = sDVE.v
                    xcv_done = sACT.v
                    if rc and k + 1 < NCH:
                        P.wait("sp", sDVE, conv_dve_done)
                        for g in range(4):
                            P.op("sp", lambda E, k2=k + 1, gg=g: E.dma_start(
                                out=wbufv[:, 0:NDT * WTS].rearrange(
                                    "p (dt t) -> p dt t", dt=NDT)
                                [:, gg * 4:(gg + 1) * 4, :],
                                in_=xdt_pre.rearrange("(dt p) t -> p dt t", p=128)
                                [:, gg * 4:(gg + 1) * 4,
                                 k2 * TC:k2 * TC + WTS]),
                                incs=[(sRC, 16)])
                    P.wait("pe", sACT, xcv_done)
                    P.wait("pe", sW, setup_w)
                    for dti in range(NDT):
                        P.op("pe", lambda E, d=dti: E.matmul(
                            p_proj[:], sxproj[:, d * 160:d * 160 + 128],
                            xcv[:, d * TC:(d + 1) * TC],
                            start=(d == 0), stop=(d == NDT - 1)),
                            incs=[(sPE, 1)] if dti == NDT - 1 else ())
                    for dti in range(NDT):
                        P.op("pe", lambda E, d=dti: E.matmul(
                            p_bc[0:16, :], sxproj[:, d * 160 + 128:d * 160 + 144],
                            xcv[:, d * TC:(d + 1) * TC],
                            start=(d == 0), stop=(d == NDT - 1)),
                            incs=[(sPE, 1)] if dti == NDT - 1 else ())
                    if not rc:
                        for dti in range(NDT):
                            P.op("pe", lambda E, d=dti: E.matmul(
                                p_bc[32:48, :],
                                sxproj[:, d * 160 + 144:d * 160 + 160],
                                xcv[:, d * TC:(d + 1) * TC],
                                start=(d == 0), stop=(d == NDT - 1)),
                                incs=[(sPE, 1)] if dti == NDT - 1 else ())
                    P.wait("act", sPE)
                    P.op("act", lambda E: E.copy(sdtin[:], p_proj[:]))
                    P.op("act", lambda E: E.copy(sbc[0:16, :], p_bc[0:16, :]))
                    if not rc:
                        P.op("act", lambda E: E.copy(sbc[32:48, :], p_bc[32:48, :]))
                    bc_done = sACT.v
                    P.wait("pe", sACT, bc_done)
                    sel = [(n, 0, Bb) for n in range(NST)]
                    if not rc:
                        sel = [x for n in range(NST)
                               for x in ((n, 0, Bb), (n, 32, Cb))]
                    copy_dones = []
                    for i, (n, row, dstT) in enumerate(sel):
                        pb = p_bb0 if i % 2 == 0 else p_bb1
                        if i >= 2:
                            P.wait("pe", sDVE, copy_dones[i - 2])
                        P.op("pe", lambda E, nn=n, r=row, pp=pb: E.matmul(
                            pp[:], seye[r:r + 16, nn * 128:(nn + 1) * 128],
                            sbc[r:r + 16, :],
                            start=True, stop=True), incs=[(sPE, 1)])
                        P.wait("dve", sPE)
                        P.op("dve", lambda E, nn=n, pp=pb, dd=dstT: E.tensor_copy(
                            dd[:, nn * TC:(nn + 1) * TC], pp[:]))
                        copy_dones.append(sDVE.v)
                    bc_copy_done[0] = sDVE.v
                    for dti in range(NDT):
                        pdl = p_dl0 if dti % 2 == 0 else p_dl1
                        P.wait("pe", sACT)
                        P.op("pe", lambda E, d=dti, pp=pdl: E.matmul(
                            pp[:], sdtproj[:, d * 128:(d + 1) * 128], sdtin[:],
                            start=True, stop=True), incs=[(sPE, 1)])
                        P.wait("act", sPE)
                        P.wait("act", sDVE, ssp_free[0])
                        P.op("act", lambda E, d=dti, pp=pdl: E.activation(
                            ssp[:, 0:TC], pp[:], AF.Abs, bias=sdtpb[:, d:d + 1]))
                        P.fence("act")
                        P.op("act", lambda E: E.activation(
                            ssp[:, 0:TC], ssp[:, 0:TC], AF.Exp, scale=-1.0))
                        P.fence("act")
                        P.op("act", lambda E: E.activation(
                            ssp[:, TC:2 * TC], ssp[:, 0:TC], AF.Ln, bias=1.0))
                        P.fence("act")
                        P.op("act", lambda E, d=dti, pp=pdl: E.activation(
                            ssp[:, 0:TC], pp[:], AF.Relu, bias=sdtpb[:, d:d + 1]))
                        P.wait("dve", sACT)
                        P.wait("dve", sPOOL)
                        P.op("dve", lambda E: E.tensor_add(
                            sdelta[:], ssp[:, 0:TC], ssp[:, TC:2 * TC]))
                        delta_done = sDVE.v
                        ssp_free[0] = delta_done
                        P.fence("pool")
                        P.wait("pool", sDVE, delta_done)
                        P.wait("pool", sACT, xcv_done)
                        P.op("pool", lambda E, d=dti: E.tensor_tensor(
                            out=sdx[:], in0=sdelta[:],
                            in1=xcv[:, d * TC:(d + 1) * TC], op=ALU.mult))
                        for h in range(2):
                            P.wait("act", sDVE, delta_done)
                            P.wait("act", sDVE, ta_free[h])
                            for n in range(NH):
                                g = dti * NST + h * NH + n
                                P.op("act", lambda E, nn=n, gg=g, hh=h:
                                     E.activation(
                                         tap[hh][:, nn * TC:(nn + 1) * TC],
                                         sdelta[:], AF.Exp,
                                         scale=sA[:, gg:gg + 1]))
                            exps_done = sACT.v
                            P.fence("pool")
                            P.wait("pool", sDVE, tb_free[h])
                            P.wait("pool", sDVE, bc_copy_done[0])
                            for n in range(NH):
                                g = h * NH + n
                                P.op("pool", lambda E, nn=n, gg=g, hh=h:
                                     E.tensor_tensor(
                                         out=tbp[hh][:, nn * TC:(nn + 1) * TC],
                                         in0=sdx[:],
                                         in1=Bb[:, gg * TC:(gg + 1) * TC],
                                         op=ALU.mult))
                            pool_done = sPOOL.v
                            cbase = dti * NST + h * NH
                            P.wait("dve", sACT, exps_done)
                            P.wait("dve", sPOOL, pool_done)
                            P.fence("dve")
                            P.op("dve", lambda E, hh=h, cb=cbase: E.tensor_tensor(
                                out=sinj[:], in0=ta3[hh][:, :, 0],
                                in1=carry[:, cb:cb + NH], op=ALU.mult))
                            P.fence("dve")
                            P.op("dve", lambda E, hh=h: E.tensor_tensor(
                                out=tb3[hh][:, :, 0], in0=tb3[hh][:, :, 0],
                                in1=sinj[:], op=ALU.add))
                            P.op("dve", lambda E, hh=h: E.tensor_scalar_mul(
                                ta3[hh][:, :, 0], ta3[hh][:, :, 0], 0.0))
                            P.fence("dve")
                            P.op("dve", lambda E, hh=h: E.tensor_tensor_scan(
                                out=tbp[hh][:], data0=tap[hh][:],
                                data1=tbp[hh][:],
                                initial=scv[:, C_ZERO:C_ZERO + 1],
                                op0=ALU.mult, op1=ALU.add))
                            P.fence("dve")
                            if not rc:
                                P.op("dve", lambda E, hh=h: E.tensor_mul(
                                    tap[hh][:], tbp[hh][:],
                                    Cb[:, hh * NH * TC:(hh + 1) * NH * TC]))
                            P.op("dve", lambda E, hh=h, cb=cbase: E.tensor_copy(
                                carry[:, cb:cb + NH], tb3[hh][:, :, TC - 1]))
                            tb_free[h] = sDVE.v
                            if rc:
                                ta_free[h] = sDVE.v
                                continue
                            for w in (4 * TC, 2 * TC, TC):
                                P.fence("dve")
                                P.op("dve", lambda E, ww=w, hh=h: E.tensor_add(
                                    tap[hh][:, 0:ww], tap[hh][:, 0:ww],
                                    tap[hh][:, ww:2 * ww]))
                            if h == 0:
                                P.fence("dve")
                                P.op("dve", lambda E: E.tensor_copy(
                                    y0[:], tap[0][:, 0:TC]))
                                ta_free[0] = sDVE.v
                        if rc:
                            continue
                        xoff = dti * XOS + PADW + base_t
                        if k < NCH - 1:
                            P.fence("dve")
                            P.op("dve", lambda E, o=xoff, d=dti: E.tensor_copy(
                                hbuf[:, d * 3:d * 3 + 3],
                                xo[:, o + TC - PADW:o + TC]))
                        P.fence("dve")
                        P.op("dve", lambda E, o=xoff, d=dti: E.scalar_tensor_tensor(
                            xo[:, o:o + TC], xo[:, o:o + TC], sd1p[:, d:d + 1],
                            tap[1][:, 0:TC], ALU.mult, ALU.add))
                        ta_free[1] = sDVE.v
                        P.fence("dve")
                        P.op("dve", lambda E, o=xoff: E.tensor_add(
                            xo[:, o:o + TC], xo[:, o:o + TC], y0[:]))
                        xo_done = sDVE.v
                        P.wait("pool", sDVE, xo_done)
                        P.wait("pool", sPE, srb_free[0])
                        P.op("pool", lambda E, o=xoff: E.tensor_copy(
                            srbq[:, 0:TC], xo[:, o:o + TC]))
                        srb_copy = sPOOL.v
                        P.wait("act", sDVE, xo_done)
                        P.wait("act", sPE, srb_free[0])
                        P.op("act", lambda E, o=xoff: E.activation(
                            srbq[:, TC:2 * TC], xo[:, o:o + TC], AF.Square))
                        sq_done = sACT.v
                        P.wait("pe", sPOOL, srb_copy)
                        P.wait("pe", sACT, sq_done)
                        P.op("pe", lambda E, d=dti: E.matmul(
                            p_st[0:1, :], sonesb[:], srbq[:],
                            start=(d == 0), stop=(d == NDT - 1)),
                            incs=[(sPE, 1)])
                        srb_free[0] = sPE.v
                    if not rc:
                        P.wait("dve", sPE)
                        P.op("dve", lambda E: E.tensor_scalar_mul(
                            sst[0:1, 0:TC], p_st[0:1, 0:TC], 1.0 / D))
                        P.op("dve", lambda E: E.tensor_scalar_mul(
                            sst[0:1, TC:2 * TC], p_st[0:1, TC:2 * TC], 1.0 / D))
                        P.fence("dve")
                        P.op("dve", lambda E: E.tensor_mul(
                            sst[0:1, 2 * TC:3 * TC], sst[0:1, 0:TC],
                            sst[0:1, 0:TC]))
                        P.fence("dve")
                        P.op("dve", lambda E: E.tensor_sub(
                            sst[0:1, TC:2 * TC], sst[0:1, TC:2 * TC],
                            sst[0:1, 2 * TC:3 * TC]))
                        P.wait("act", sDVE)
                        P.op("act", lambda E: E.activation(
                            sst[0:1, 2 * TC:3 * TC], sst[0:1, TC:2 * TC], AF.Ln,
                            bias=scv[0:1, C_EPS:C_EPS + 1]))
                        P.fence("act")
                        P.op("act", lambda E: E.activation(
                            sst[0:1, TC:2 * TC], sst[0:1, 2 * TC:3 * TC], AF.Exp,
                            scale=scv[0:1, C_NEGHALF:C_NEGHALF + 1]))
                        P.wait("pe", sACT)
                        P.op("pe", lambda E: E.matmul(
                            p_bcst[:, 0:TC], sones1[:], sst[0:1, 0:TC],
                            start=True, stop=True))
                        P.op("pe", lambda E: E.matmul(
                            p_bcst[:, TC:2 * TC], sones1[:], sst[0:1, TC:2 * TC],
                            start=True, stop=True), incs=[(sPE, 1)])
                        P.wait("dve", sPE)
                        for dti in range(NDT):
                            xoff = dti * XOS + PADW + base_t
                            P.op("dve", lambda E, o=xoff: E.tensor_sub(
                                xo[:, o:o + TC], xo[:, o:o + TC], p_bcst[:, 0:TC]))
                            P.fence("dve")
                            P.op("dve", lambda E, o=xoff: E.tensor_mul(
                                xo[:, o:o + TC], xo[:, o:o + TC],
                                p_bcst[:, TC:2 * TC]))
                            mul_done = sDVE.v
                            P.wait("pool", sDVE, mul_done)
                            P.op("pool", lambda E, o=xoff, d=dti: E.tensor_copy(
                                xb[:, d * MC + base_t:d * MC + base_t + TC],
                                xo[:, o:o + TC]))
                    barc["pe"] = sPE.v
                    barc["act"] = sACT.v
                    barc["dve"] = sDVE.v
                    barc["pool"] = sPOOL.v

                for g in range(4):
                    P.op("sp", lambda E, gg=g: E.dma_start(
                        out=wbufv[:, 0:NDT * WTS].rearrange(
                            "p (dt t) -> p dt t", dt=NDT)
                        [:, gg * 4:(gg + 1) * 4, :],
                        in_=xdt_pre.rearrange("(dt p) t -> p dt t", p=128)
                        [:, gg * 4:(gg + 1) * 4, 0:WTS]),
                        incs=[(sRC, 16)])
                for kk in range(NCH):
                    mamba_chunk(True, kk)
                P.wait("dve", sIN, setup_in)
                P.fence("dve")
                P.op("dve", lambda E: E.tensor_scalar_mul(
                    carry[:], carry[:], srcm[:, 0:1]))
                P.fence("dve")
                for kk in range(NCH):
                    mamba_chunk(False, kk)

                if debug:
                    bwait("sp")
                    for dti in range(NDT):
                        P.op("sp", lambda E, d=dti: E.dma_start(
                            out=dbg_xin[:, d * MC:(d + 1) * MC],
                            in_=xo[:, d * XOS + PADW:d * XOS + PADW + MC]),
                            incs=[(sOUT, 16)])

            acc = sbuf("acc", [128, NTT * D], F32)
            xloq = sbuf("xloq", [128, 4 * 128], BF16)
            sbt = sbuf("sbt", [128, 512], BF16)
            with ExitStack() as tstack:
                p_lg = tstack.enter_context(nc.psum_tensor("p_lg", [128, 8], F32))
                p_tr = tstack.enter_context(
                    nc.psum_tensor("p_tr", [128, 1024], BF16))

                bwait("pe")
                bwait("act")
                bwait("dve")
                for tt in range(NTT):
                    for dti in range(NDT):
                        P.op("pe", lambda E, d=dti, t=tt: E.matmul(
                            p_lg[:],
                            xo[:, d * XOS + PADW + t * 128:
                               d * XOS + PADW + (t + 1) * 128],
                            swg[:, d * 8:(d + 1) * 8],
                            start=(d == 0), stop=(d == NDT - 1)),
                            incs=[(sPE, 1)] if dti == NDT - 1 else ())

                    def r0(c):
                        return lsc[:, c * 8:(c + 1) * 8]
                    P.wait("dve", sPE)
                    P.op("dve", lambda E: E.reduce_max(lsc[:, 64:65], p_lg[:], AX.X))
                    P.fence("dve")
                    P.op("dve", lambda E: E.tensor_scalar_mul(
                        lsc[:, 65:66], lsc[:, 64:65], -1.0))
                    P.op("dve", lambda E: E.tensor_copy(r0(0), p_lg[:]))
                    P.wait("act", sDVE)
                    P.op("act", lambda E: E.activation(
                        r0(1), r0(0), AF.Exp, bias=lsc[:, 65:66]))
                    P.wait("dve", sACT)
                    P.op("dve", lambda E: E.reduce_max(lsc[:, 64:65], r0(1), AX.X))
                    P.fence("dve")
                    P.op("dve", lambda E: E.tensor_single_scalar(
                        r0(2), r0(1), lsc[:, 64:65], ALU.is_equal))
                    P.fence("dve")
                    P.op("dve", lambda E: E.tensor_scalar(
                        r0(3), r0(2), -1.0, 1.0, ALU.mult, ALU.add))
                    P.fence("dve")
                    P.op("dve", lambda E: E.tensor_mul(r0(3), r0(3), r0(1)))
                    P.fence("dve")
                    P.op("dve", lambda E: E.reduce_max(lsc[:, 66:67], r0(3), AX.X))
                    P.fence("dve")
                    P.op("dve", lambda E: E.tensor_single_scalar(
                        r0(4), r0(3), lsc[:, 66:67], ALU.is_equal))
                    P.fence("dve")
                    P.op("dve", lambda E: E.tensor_add(r0(2), r0(2), r0(4)))
                    P.op("dve", lambda E: E.tensor_add(
                        lsc[:, 67:68], lsc[:, 64:65], lsc[:, 66:67]))
                    P.fence("dve")
                    P.op("dve", lambda E: E.reciprocal(lsc[:, 68:69], lsc[:, 67:68]))
                    P.fence("dve")
                    P.op("dve", lambda E: E.tensor_mul(r0(2), r0(2), r0(1)))
                    P.fence("dve")
                    P.op("dve", lambda E, t=tt: E.tensor_scalar_mul(
                        comb[:, t * 8:(t + 1) * 8], r0(2), lsc[:, 68:69]))
                    P.wait("pe", sDVE)

                if debug:
                    P.wait("sp", sDVE)
                    P.op("sp", lambda E: E.dma_start(out=dbg_comb, in_=comb[:]),
                         incs=[(sOUT, 16)])

                for tt in range(NTT):
                    for dq in range(4):
                        P.wait("dve", sPE)
                        for i in range(4):
                            dti = dq * 4 + i
                            xoff = dti * XOS + PADW + tt * 128
                            P.op("dve", lambda E, o=xoff, d=dti, ii=i, t=tt:
                                 E.tensor_sub(
                                     xloq[:, ii * 128:(ii + 1) * 128],
                                     xo[:, o:o + 128],
                                     xb[:, d * MC + t * 128:d * MC + t * 128 + 128]))
                        P.wait("pe", sDVE)
                        for i in range(4):
                            dti = dq * 4 + i
                            P.op("pe", lambda E, d=dti, ii=i, t=tt: E.transpose(
                                p_tr[:, ii * 128:(ii + 1) * 128],
                                xb[:, d * MC + t * 128:d * MC + t * 128 + 128],
                                sident[:]))
                            P.op("pe", lambda E, ii=i: E.transpose(
                                p_tr[:, 512 + ii * 128:512 + (ii + 1) * 128],
                                xloq[:, ii * 128:(ii + 1) * 128], sident[:]),
                                incs=[(sPE, 1)] if i == 3 else ())
                        P.wait("dve", sPE)
                        P.op("dve", lambda E: E.tensor_copy(sbt[:], p_tr[:, 0:512]))
                        P.fence("dve")
                        P.op("dve", lambda E, t=tt, q=dq: E.tensor_add(
                            acc[:, t * D + q * 512:t * D + (q + 1) * 512],
                            sbt[:], p_tr[:, 512:1024]))
                seed_dve = sDVE.v

        H = sbuf("H", [128, NF * MC], BF16, side="right")
        wgb = [sbuf("wgb0", [128, D], BF16), sbuf("wgb1", [128, D], BF16)]
        wub = [sbuf("wub0", [128, D], BF16), sbuf("wub1", [128, D], BF16)]
        wdb = [sbuf("wdb0", [128, D], BF16), sbuf("wdb1", [128, D], BF16)]
        tsil = sbuf("tsil", [128, MC], F32)
        sq16 = sbuf("sq16", [128, D], BF16)
        pG = en(nc.psum_tensor("pG", [128, MC], F32))
        pU = en(nc.psum_tensor("pU", [128, MC], F32))
        pdA = en(nc.psum_tensor("pdA", [128, MC], F32))
        pdB = en(nc.psum_tensor("pdB", [128, MC], F32))

        def pd_slice(blk, tt):
            grpA = (blk % 2 == 0)
            if tt < 4:
                base = pG if grpA else pdA
            else:
                base = pU if grpA else pdB
            c = (tt % 4) * 256
            return base[:, c:c + 256]

        ngl = [0]
        ndl = [0]
        silu_cap = []
        hm_cap = []
        ev_cap = []

        for e in range(n_experts):
            for f in range(NF):
                fgi = e * NF + f
                par = f % 2
                if ngl[0] == 0:
                    P.wait("pool", sDVE, seed_dve)
                if ngl[0] >= 2:
                    P.wait("pool", sGf, ngl[0] - 1)
                    P.wait("pool", sUf, ngl[0] - 1)
                row = (e * NF + f) * 128
                P.op("pool", lambda E, r=row, p=par: E.dma_start(
                    out=wgb[p][:], in_=gate_w[r:r + 128, :]),
                    incs=[(sWg[par], 16)])
                P.op("pool", lambda E, r=row, p=par: E.dma_start(
                    out=wub[p][:], in_=up_w[r:r + 128, :]),
                    incs=[(sWu[par], 16)])
                ngl[0] += 1
                P.wait("pe", sWg[par], sWg[par].v)
                if fgi > 0:
                    P.wait("pe", sACT, silu_cap[fgi - 1])
                if e == 0 and f == 0:
                    P.wait("pe", sDVE, seed_dve)
                if f == 0 and e > 0:
                    P.wait("pe", sDVE, ev_cap[-1])
                for dti in range(NDT):
                    for h in range(2):
                        P.op("pe", lambda E, d=dti, hh=h, p=par: E.matmul(
                            pG[:, hh * 512:(hh + 1) * 512],
                            wgb[p][:, d * 128:(d + 1) * 128],
                            xb[:, d * MC + hh * 512:d * MC + (hh + 1) * 512],
                            start=(d == 0), stop=(d == NDT - 1)),
                            incs=[(sGf, 1)] if (dti == NDT - 1 and h == 1) else ())
                P.wait("pe", sWu[par], sWu[par].v)
                if fgi > 0:
                    P.wait("pe", sDVE, hm_cap[fgi - 1])
                for dti in range(NDT):
                    for h in range(2):
                        P.op("pe", lambda E, d=dti, hh=h, p=par: E.matmul(
                            pU[:, hh * 512:(hh + 1) * 512],
                            wub[p][:, d * 128:(d + 1) * 128],
                            xb[:, d * MC + hh * 512:d * MC + (hh + 1) * 512],
                            start=(d == 0), stop=(d == NDT - 1)),
                            incs=[(sUf, 1)] if (dti == NDT - 1 and h == 1) else ())
                P.wait("act", sGf, fgi + 1)
                if fgi > 0:
                    P.wait("act", sDVE, hm_cap[fgi - 1])
                P.op("act", lambda E: E.activation(tsil[:], pG[:], AF.Silu))
                silu_cap.append(sACT.v)
                P.wait("dve", sACT, silu_cap[fgi])
                P.wait("dve", sUf, sUf.v)
                P.op("dve", lambda E, ff=f: E.tensor_tensor(
                    H[:, ff * MC:(ff + 1) * MC], tsil[:], pU[:], ALU.mult))
                hm_cap.append(sDVE.v)
            P.wait("pe", sDVE, hm_cap[-1])
            for blk in range(8):
                for fq in range(4):
                    if ndl[0] >= 2:
                        P.wait("pool", sDf, ndl[0] - 1)
                    rowd = (e * NF + blk * 4 + fq) * 128
                    pard = ndl[0] % 2
                    P.op("pool", lambda E, r=rowd, p=pard: E.dma_start(
                        out=wdb[p][:], in_=down_w[r:r + 128, :]),
                        incs=[(sWd[pard], 16)])
                    ndl[0] += 1
                    P.wait("pe", sWd[pard], sWd[pard].v)
                    if fq == 0 and len(ev_cap) >= 2:
                        P.wait("pe", sDVE, ev_cap[-2])
                    for f8 in range(8):
                        f = fq * 8 + f8
                        for tt in range(NTT):
                            last_piece = (f8 == 7 and tt == NTT - 1)
                            P.op("pe", lambda E, ff=f, t=tt, b=blk, p=pard, q=fq,
                                 f8_=f8: E.matmul(
                                     pd_slice(b, t),
                                     H[:, ff * MC + t * 128:ff * MC + (t + 1) * 128],
                                     wdb[p][:, f8_ * 256:(f8_ + 1) * 256],
                                     start=(q == 0 and f8_ == 0 and t % 2 == 0),
                                     stop=(q == 3 and f8_ == 7)),
                                 incs=[(sDf, 1)] if last_piece else ())
                P.wait("dve", sDf, ndl[0])
                for tt in range(NTT):
                    P.fence("dve")
                    if e == 0:
                        P.op("dve", lambda E, b=blk, t=tt: E.scalar_tensor_tensor(
                            acc[:, t * D + b * 256:t * D + (b + 1) * 256],
                            pd_slice(b, t), 1.0,
                            acc[:, t * D + b * 256:t * D + (b + 1) * 256],
                            ALU.mult, ALU.add))
                    else:
                        P.op("dve", lambda E, b=blk, t=tt, ee=e:
                             E.scalar_tensor_tensor(
                                 acc[:, t * D + b * 256:t * D + (b + 1) * 256],
                                 pd_slice(b, t),
                                 comb[:, t * 8 + ee - 1:t * 8 + ee],
                                 acc[:, t * D + b * 256:t * D + (b + 1) * 256],
                                 ALU.mult, ALU.add))
                ev_cap.append(sDVE.v)

        P.wait("dve", sDVE, ev_cap[-1])
        for tt in range(NTT):
            a_tt = acc[:, tt * D:(tt + 1) * D]
            P.fence("dve")
            P.op("dve", lambda E, a=a_tt: E.reduce_sum(lsc[:, 64:65], a, AX.X))
            P.fence("dve")
            P.op("dve", lambda E: E.tensor_scalar_mul(
                lsc[:, 64:65], lsc[:, 64:65], 1.0 / D))
            P.fence("dve")
            P.op("dve", lambda E, a=a_tt: E.tensor_single_scalar(
                a, a, lsc[:, 64:65], ALU.subtract))
            P.wait("act", sDVE)
            P.op("act", lambda E, a=a_tt: E.activation(sq16[:], a, AF.Square))
            P.wait("dve", sACT)
            P.op("dve", lambda E: E.reduce_sum(lsc[:, 65:66], sq16[:], AX.X))
            P.fence("dve")
            P.op("dve", lambda E: E.tensor_scalar_mul(
                lsc[:, 65:66], lsc[:, 65:66], 1.0 / D))
            P.wait("act", sDVE)
            P.op("act", lambda E: E.activation(
                lsc[:, 66:67], lsc[:, 65:66], AF.Ln,
                bias=scv[:, C_EPS:C_EPS + 1]))
            P.fence("act")
            P.op("act", lambda E: E.activation(
                lsc[:, 67:68], lsc[:, 66:67], AF.Exp,
                scale=scv[:, C_NEGHALF:C_NEGHALF + 1]))
            P.wait("dve", sACT)
            P.op("dve", lambda E, a=a_tt: E.tensor_scalar_mul(
                a, a, lsc[:, 67:68]))
            P.wait("sp", sDVE)
            P.op("sp", lambda E, t=tt, a=a_tt: E.dma_start(
                out=out_hbm[t * 128:(t + 1) * 128, :], in_=a),
                incs=[(sOUT, 16)])
        fin = sOUT.v
        P.q["sp"].append(lambda E, t=fin: E.wait_ge(sOUT.h, t))

        with nc.Block() as block:
            @block.tensor
            def _(E):
                for fn in P.q["pe"]:
                    fn(E)

            @block.scalar
            def _(E):
                for fn in P.q["act"]:
                    fn(E)

            @block.vector
            def _(E):
                for fn in P.q["dve"]:
                    fn(E)

            @block.gpsimd
            def _(E):
                for fn in P.q["pool"]:
                    fn(E)

            @block.sync
            def _(E):
                for fn in P.q["sp"]:
                    fn(E)

    return nc



def _prep_shared(i):
    f32 = np.float32
    import ml_dtypes
    sh = {}
    sh["cvec"] = np.tile(np.array([[EPS, -0.5, 0.0, 0.0]], dtype=f32), (128, 1))
    cw = np.ascontiguousarray(np.asarray(i["L0_attn_conv_w"], f32)[:, 0, :])
    sh["convw_t"] = np.ascontiguousarray(
        cw.reshape(NDT, 128, 4).transpose(1, 0, 2).reshape(128, NDT * 4))
    for name, vec in [("convb_t", i["L0_attn_conv_b"]),
                      ("dtprojb_t", i["L0_attn_dtproj_b"]),
                      ("d1p_t", 1.0 + np.asarray(i["L0_attn_D_param"], f32))]:
        sh[name] = np.ascontiguousarray(
            np.asarray(vec, dtype=f32).reshape(NDT, 128).T)
    sh["alog_t"] = np.ascontiguousarray(
        np.asarray(i["L0_attn_A_log"], f32).reshape(NDT, 128, NST)
        .transpose(1, 0, 2).reshape(128, NDT * NST))
    xp = np.asarray(i["L0_attn_xproj_w"], f32)
    sh["xproj_t"] = np.ascontiguousarray(
        xp.T.reshape(NDT, 128, 160).transpose(1, 0, 2).reshape(128, NDT * 160))
    sh["dtproj_t"] = np.ascontiguousarray(np.asarray(i["L0_attn_dtproj_w"], f32).T)
    wg = np.asarray(i["L0_ffn_wg"], f32)
    sh["wg_t"] = np.ascontiguousarray(
        wg.T.reshape(NDT, 128, 8).transpose(1, 0, 2).reshape(128, NDT * 8))
    eye = np.zeros((48, NST * 128), dtype=f32)
    for n in range(NST):
        eye[n, n * 128:(n + 1) * 128] = 1.0
        eye[32 + n, n * 128:(n + 1) * 128] = 1.0
    sh["eye16"] = eye.astype(ml_dtypes.bfloat16)
    sh["ones1_128"] = np.ones((1, 128), dtype=f32)
    sh["ones128_bf"] = np.ones((128, 1), dtype=ml_dtypes.bfloat16)
    sh["ident_bf"] = np.eye(128, dtype=f32).astype(ml_dtypes.bfloat16)

    def tile_gu(mats):
        out = np.empty((NE * NF * 128, D), dtype=f32)
        for e, W in enumerate(mats):
            t = W.reshape(NF, 128, NDT, 128).transpose(0, 3, 2, 1)
            out[e * NF * 128:(e + 1) * NF * 128] = t.reshape(NF * 128, D)
        return out

    gmats = [np.asarray(i["L0_ffn_sh_gate"], f32)] + \
            [np.asarray(i["L0_ffn_ex_gate"][e], f32) for e in range(8)]
    umats = [np.asarray(i["L0_ffn_sh_up"], f32)] + \
            [np.asarray(i["L0_ffn_ex_up"][e], f32) for e in range(8)]
    sh["gate_w"] = tile_gu(gmats)
    sh["up_w"] = tile_gu(umats)

    def tile_down(mats):
        out = np.empty((NE * NF * 128, D), dtype=f32)
        for e, Wd in enumerate(mats):
            t = Wd.reshape(8, 256, 4, 8, 128)
            t = t.transpose(0, 2, 4, 3, 1)
            out[e * NF * 128:(e + 1) * NF * 128] = t.reshape(NF * 128, D)
        return out

    dmats = [np.asarray(i["L0_ffn_sh_down"], f32)] + \
            [np.asarray(i["L0_ffn_ex_down"][e], f32) for e in range(8)]
    sh["down_w"] = tile_down(dmats)
    return sh


def _prep_core(x, b, h):
    f32 = np.float32
    xt = np.asarray(x[b], f32).T
    own = np.zeros((D, XOS), dtype=f32)
    own[:, PADW:] = xt[:, h * MC:(h + 1) * MC]
    if h == 1:
        own[:, 0:PADW] = xt[:, MC - PADW:MC]
    pre = np.zeros((D, XOS), dtype=f32)
    if h == 1:
        pre[:, PADW:] = xt[:, 0:MC]
    return np.ascontiguousarray(own), np.ascontiguousarray(pre)


def kernel(**inputs):
    x = np.asarray(inputs["L0_attn_x"], np.float32)
    sh = _prep_shared(inputs)
    nc = build_program(debug=False)
    in_maps = []
    for core in range(8):
        b, h = core // 2, core % 2
        own, pre = _prep_core(x, b, h)
        m = dict(sh)
        m["xdt_own"] = own
        m["xdt_pre"] = pre
        m["rcmask"] = np.full((128, 1), 1.0 if h == 1 else 0.0, np.float32)
        in_maps.append(m)
    res = run_bass_kernel_spmd(nc, in_maps, core_ids=list(range(8)))
    out = np.empty((4, 2048, D), dtype=np.float32)
    for core in range(8):
        b, h = core // 2, core % 2
        out[b, h * MC:(h + 1) * MC, :] = res.results[core]["out"]
    return out


# version 06
# speedup vs baseline: 1.0327x; 1.0078x over previous
import numpy as np
from contextlib import ExitStack
import concourse.bass as bass
import concourse.mybir as mybir
from concourse.bass_utils import run_bass_kernel_spmd

F32 = mybir.dt.float32
BF16 = mybir.dt.bfloat16
AF = mybir.ActivationFunctionType
ALU = mybir.AluOpType
AX = mybir.AxisListType

D = 2048
NDT = 16
MC = 1024
TC = 256
NCH = MC // TC
NST = 16
DTR = 128
FF = 4096
NF = 32
NE = 9
PADW = 3
XOS = PADW + MC
WTS = TC + PADW
NTT = 8
EPS = 1e-5

C_EPS = 0
C_NEGHALF = 1
C_ZERO = 2
C_ONE = 3
C_NCONST = 4
NH = NST // 2


class S:
    def __init__(self, stack, nc, name):
        self.h = stack.enter_context(nc.semaphore(name))
        self.v = 0


class Prog:

    ENGS = ("pe", "act", "dve", "pool", "sp")

    def __init__(self):
        self.q = {e: [] for e in self.ENGS}
        self.auto = {}

    def wait(self, eng, s, at=None):
        tgt = s.v if at is None else at
        if tgt > 0:
            self.q[eng].append(lambda E, h=s.h, t=tgt: E.wait_ge(h, t))

    def op(self, eng, fn, incs=(), noauto=False):
        a = self.auto.get(eng)
        if a is not None and not incs and not noauto:
            incs = ((a, 1),)
        if incs:
            def wrapped(E, f=fn, pairs=tuple((s.h, n) for s, n in incs)):
                ins = f(E)
                for h, n in pairs:
                    ins.then_inc(h, n)
                return ins
            self.q[eng].append(wrapped)
            for s, n in incs:
                s.v += n
        else:
            self.q[eng].append(fn)

    def fence(self, eng):
        self.wait(eng, self.auto[eng])


def build_program(debug=False, n_experts=NE):
    nc = bass.Bass("TRN2", target_bir_lowering=False, debug=False)
    dtr = nc.dram_tensor

    xdt_own = dtr("xdt_own", [D, XOS], F32, kind="ExternalInput").ap()
    xdt_pre = dtr("xdt_pre", [D, XOS], F32, kind="ExternalInput").ap()
    rcmask = dtr("rcmask", [128, 1], F32, kind="ExternalInput").ap()
    cvec = dtr("cvec", [128, C_NCONST], F32, kind="ExternalInput").ap()
    convw_t = dtr("convw_t", [128, NDT * 4], F32, kind="ExternalInput").ap()
    convb_t = dtr("convb_t", [128, NDT], F32, kind="ExternalInput").ap()
    dtprojb_t = dtr("dtprojb_t", [128, NDT], F32, kind="ExternalInput").ap()
    d1p_t = dtr("d1p_t", [128, NDT], F32, kind="ExternalInput").ap()
    alog_t = dtr("alog_t", [128, NDT * NST], F32, kind="ExternalInput").ap()
    xproj_t = dtr("xproj_t", [128, NDT * 160], F32, kind="ExternalInput").ap()
    dtproj_t = dtr("dtproj_t", [DTR, D], F32, kind="ExternalInput").ap()
    wg_t = dtr("wg_t", [128, NDT * 8], F32, kind="ExternalInput").ap()
    eye16 = dtr("eye16", [48, NST * 128], BF16, kind="ExternalInput").ap()
    ones1_128 = dtr("ones1_128", [1, 128], F32, kind="ExternalInput").ap()
    ones128_bf = dtr("ones128_bf", [128, 1], BF16, kind="ExternalInput").ap()
    ident_bf = dtr("ident_bf", [128, 128], BF16, kind="ExternalInput").ap()
    gate_w = dtr("gate_w", [NE * NF * 128, D], F32, kind="ExternalInput").ap()
    up_w = dtr("up_w", [NE * NF * 128, D], F32, kind="ExternalInput").ap()
    down_w = dtr("down_w", [NE * NF * 128, D], F32, kind="ExternalInput").ap()
    out_hbm = dtr("out", [MC, D], F32, kind="ExternalOutput").ap()
    if debug:
        dbg_xin = dtr("dbg_xin", [128, NDT * MC], F32, kind="ExternalOutput").ap()
        dbg_comb = dtr("dbg_comb", [128, NTT * 8], F32, kind="ExternalOutput").ap()

    P = Prog()

    with ExitStack() as stack:
        en = stack.enter_context

        def sbuf(name, shape, dtype, side="left"):
            return en(nc.sbuf_tensor(name, shape, dtype, side=side))

        sIN = S(stack, nc, "sIN")
        sRC = S(stack, nc, "sRC")
        sOUT = S(stack, nc, "sOUT")
        sW = S(stack, nc, "sW")
        sWg = [S(stack, nc, "sWg0"), S(stack, nc, "sWg1")]
        sWu = [S(stack, nc, "sWu0"), S(stack, nc, "sWu1")]
        sWd = [S(stack, nc, "sWd0"), S(stack, nc, "sWd1")]
        sPE = S(stack, nc, "sPE")
        sACT = S(stack, nc, "sACT")
        sDVE = S(stack, nc, "sDVE")
        sGf = S(stack, nc, "sGf")
        sUf = S(stack, nc, "sUf")
        sDf = S(stack, nc, "sDf")
        sPOOL = S(stack, nc, "sPOOL")
        P.auto = {"dve": sDVE, "act": sACT, "pool": sPOOL}

        scv = sbuf("scv", [128, C_NCONST], F32)
        nc.const_aps.register(F32, EPS, scv[:, C_EPS:C_EPS + 1])
        nc.const_aps.register(F32, -0.5, scv[:, C_NEGHALF:C_NEGHALF + 1])
        swg = sbuf("swg", [128, NDT * 8], F32)
        sident = sbuf("sident", [128, 128], BF16)
        xb = sbuf("xb", [128, NDT * MC], BF16)
        carry = sbuf("carry", [128, NDT * NST], F32)
        comb = sbuf("comb", [128, NTT * 8], F32)
        lsc = sbuf("lsc", [128, 80], F32)

        barc = {"pe": 0, "act": 0, "dve": 0, "pool": 0}

        def bwait(eng):
            P.wait(eng, sPE, barc["pe"])
            P.wait(eng, sACT, barc["act"])
            P.wait(eng, sDVE, barc["dve"])
            P.wait(eng, sPOOL, barc["pool"])

        with ExitStack() as xo_stack:
            xo = xo_stack.enter_context(
                nc.sbuf_tensor("xo", [128, NDT * XOS], F32, side="right"))
            with ExitStack() as mstack:
                men = mstack.enter_context

                def msbuf(name, shape, dtype):
                    return men(nc.sbuf_tensor(name, shape, dtype, side="right"))

                sconvw = msbuf("sconvw", [128, NDT * 4], F32)
                sconvb = msbuf("sconvb", [128, NDT], F32)
                sdtpb = msbuf("sdtpb", [128, NDT], F32)
                sd1p = msbuf("sd1p", [128, NDT], F32)
                sA = msbuf("sA", [128, NDT * NST], F32)
                sxproj = msbuf("sxproj", [128, NDT * 160], BF16)
                sdtproj = msbuf("sdtproj", [DTR, D], BF16)
                seye = msbuf("seye", [48, NST * 128], BF16)
                sones1 = msbuf("sones1", [1, 128], F32)
                sonesb = msbuf("sonesb", [128, 1], BF16)
                sdelta = msbuf("sdelta", [128, TC], F32)
                sdx = msbuf("sdx", [128, TC], F32)
                ssp = msbuf("ssp", [128, 4 * TC], F32)
                srq = msbuf("srq", [128, TC], F32)
                sst = msbuf("sst", [1, 3 * TC], F32)
                sdtin = msbuf("sdtin", [128, TC], BF16)
                sbc = msbuf("sbc", [64, TC], BF16)
                srcm = msbuf("srcm", [128, 1], F32)
                sinj = msbuf("sinj", [128, NH], F32)
                hbuf = msbuf("hbuf", [128, NDT * PADW], F32)
                y0 = msbuf("y0", [128, TC], F32)
                Bb = msbuf("Bb", [128, NST * TC], BF16)
                Cb = msbuf("Cb", [128, NST * TC], BF16)
                tap = [msbuf("ta0", [128, NH * TC], F32),
                       msbuf("ta1", [128, NH * TC], F32)]
                tbp = [msbuf("tb0", [128, NH * TC], F32),
                       msbuf("tb1", [128, NH * TC], F32)]
                xcv = msbuf("xcv", [128, NDT * TC], BF16)
                wbufv = xb[:].bitcast(F32)

                p_proj = men(nc.psum_tensor("p_proj", [128, TC], F32))
                p_bc = men(nc.psum_tensor("p_bc", [64, TC], F32))
                p_dl0 = men(nc.psum_tensor("p_dl0", [128, TC], F32))
                p_dl1 = men(nc.psum_tensor("p_dl1", [128, TC], F32))
                p_bb0 = men(nc.psum_tensor("p_bb0", [128, TC], F32))
                p_bb1 = men(nc.psum_tensor("p_bb1", [128, TC], F32))
                p_st = men(nc.psum_tensor("p_st", [1, 2 * TC], F32))
                p_bcst = men(nc.psum_tensor("p_bcst", [128, 2 * TC], F32))

                for dst, src in [
                    (scv[:], cvec), (sconvw[:], convw_t), (sconvb[:], convb_t),
                    (sdtpb[:], dtprojb_t), (sd1p[:], d1p_t), (sA[:], alog_t),
                    (swg[:], wg_t), (seye[:], eye16), (sones1[:], ones1_128),
                    (sonesb[:], ones128_bf), (sident[:], ident_bf),
                    (srcm[:], rcmask),
                ]:
                    P.op("sp", lambda E, d=dst, s=src: E.dma_start(out=d, in_=s),
                         incs=[(sIN, 16)])
                P.op("pool", lambda E: E.dma_start(out=sxproj[:], in_=xproj_t),
                     incs=[(sW, 16)])
                P.op("pool", lambda E: E.dma_start(out=sdtproj[:], in_=dtproj_t),
                     incs=[(sW, 16)])
                P.op("sp", lambda E: E.dma_start(
                    out=xo[:].rearrange("p (dt t) -> p dt t", dt=NDT),
                    in_=xdt_own.rearrange("(dt p) t -> p dt t", p=128)),
                    incs=[(sIN, 16)])
                setup_in = sIN.v
                setup_w = sW.v

                P.wait("act", sIN, setup_in)
                P.op("act", lambda E: E.activation(sA[:], sA[:], AF.Exp))
                P.wait("dve", sACT)
                P.op("dve", lambda E: E.tensor_scalar_mul(sA[:], sA[:], -1.0))
                P.op("dve", lambda E: E.memset(carry[:], 0.0))
                amaterial = sDVE.v

                ta3 = [t[:].rearrange("p (n t) -> p n t", n=NH) for t in tap]
                tb3 = [t[:].rearrange("p (n t) -> p n t", n=NH) for t in tbp]
                ta_free = [(sDVE, 0), (sDVE, 0)]
                tb_free = [(sDVE, 0), (sDVE, 0)]
                srb_free = [0]
                bc_copy_done = [0]

                def mamba_chunk(rc, k):
                    base_t = k * TC
                    bwait("dve")
                    bwait("act")
                    bwait("pe")
                    bwait("pool")
                    if rc:
                        P.wait("dve", sRC)
                        P.wait("dve", sDVE, amaterial)
                    else:
                        P.wait("dve", sIN, setup_in)
                    for dti in range(NDT):
                        xc = ssp[:, 0:TC]
                        if dti > 0:
                            P.wait("dve", sACT)
                        if rc or k == 0:
                            src = wbufv if rc else xo[:]
                            b0 = dti * (WTS if rc else XOS) + PADW
                            P.op("dve", lambda E, s=src, b=b0, d=dti:
                                 E.tensor_scalar_mul(
                                     xc, s[:, b - 3:b - 3 + TC],
                                     sconvw[:, d * 4:d * 4 + 1]))
                            for j in range(1, 4):
                                P.fence("dve")
                                P.op("dve", lambda E, s=src, b=b0, d=dti, jj=j:
                                     E.scalar_tensor_tensor(
                                         xc, s[:, b - 3 + jj:b - 3 + jj + TC],
                                         sconvw[:, d * 4 + jj:d * 4 + jj + 1], xc,
                                         ALU.mult, ALU.add))
                        else:
                            o = dti * XOS + PADW + base_t
                            P.op("dve", lambda E, oo=o, d=dti:
                                 E.tensor_scalar_mul(
                                     xc[:, 3:TC], xo[:, oo:oo + TC - 3],
                                     sconvw[:, d * 4:d * 4 + 1]))
                            P.op("dve", lambda E, d=dti: E.tensor_scalar_mul(
                                xc[:, 0:3], hbuf[:, d * 3:d * 3 + 3],
                                sconvw[:, d * 4:d * 4 + 1]))
                            for j in range(1, 4):
                                P.fence("dve")
                                P.op("dve", lambda E, oo=o, d=dti, jj=j:
                                     E.scalar_tensor_tensor(
                                         xc[:, 3 - jj:TC],
                                         xo[:, oo:oo + TC - 3 + jj],
                                         sconvw[:, d * 4 + jj:d * 4 + jj + 1],
                                         xc[:, 3 - jj:TC], ALU.mult, ALU.add))
                                if j < 3:
                                    P.op("dve", lambda E, d=dti, jj=j:
                                         E.scalar_tensor_tensor(
                                             xc[:, 0:3 - jj],
                                             hbuf[:, d * 3 + jj:d * 3 + 3],
                                             sconvw[:, d * 4 + jj:d * 4 + jj + 1],
                                             xc[:, 0:3 - jj], ALU.mult, ALU.add))
                        P.wait("act", sDVE)
                        P.op("act", lambda E, d=dti: E.activation(
                            xcv[:, d * TC:(d + 1) * TC], xc, AF.Silu,
                            bias=sconvb[:, d:d + 1]))
                    conv_dve_done = sDVE.v
                    xcv_done = sACT.v
                    if rc and k + 1 < NCH:
                        P.wait("sp", sDVE, conv_dve_done)
                        for g in range(4):
                            P.op("sp", lambda E, k2=k + 1, gg=g: E.dma_start(
                                out=wbufv[:, 0:NDT * WTS].rearrange(
                                    "p (dt t) -> p dt t", dt=NDT)
                                [:, gg * 4:(gg + 1) * 4, :],
                                in_=xdt_pre.rearrange("(dt p) t -> p dt t", p=128)
                                [:, gg * 4:(gg + 1) * 4,
                                 k2 * TC:k2 * TC + WTS]),
                                incs=[(sRC, 16)])
                    P.wait("pe", sACT, xcv_done)
                    P.wait("pe", sW, setup_w)
                    for dti in range(NDT):
                        P.op("pe", lambda E, d=dti: E.matmul(
                            p_proj[:], sxproj[:, d * 160:d * 160 + 128],
                            xcv[:, d * TC:(d + 1) * TC],
                            start=(d == 0), stop=(d == NDT - 1)),
                            incs=[(sPE, 1)] if dti == NDT - 1 else ())
                    for dti in range(NDT):
                        P.op("pe", lambda E, d=dti: E.matmul(
                            p_bc[0:16, :], sxproj[:, d * 160 + 128:d * 160 + 144],
                            xcv[:, d * TC:(d + 1) * TC],
                            start=(d == 0), stop=(d == NDT - 1)),
                            incs=[(sPE, 1)] if dti == NDT - 1 else ())
                    if not rc:
                        for dti in range(NDT):
                            P.op("pe", lambda E, d=dti: E.matmul(
                                p_bc[32:48, :],
                                sxproj[:, d * 160 + 144:d * 160 + 160],
                                xcv[:, d * TC:(d + 1) * TC],
                                start=(d == 0), stop=(d == NDT - 1)),
                                incs=[(sPE, 1)] if dti == NDT - 1 else ())
                    P.wait("act", sPE)
                    P.op("act", lambda E: E.copy(sdtin[:], p_proj[:]))
                    P.op("act", lambda E: E.copy(sbc[0:16, :], p_bc[0:16, :]))
                    if not rc:
                        P.op("act", lambda E: E.copy(sbc[32:48, :], p_bc[32:48, :]))
                    bc_done = sACT.v
                    P.wait("pe", sACT, bc_done)
                    sel = [(n, 0, Bb) for n in range(NST)]
                    if not rc:
                        sel = [x for n in range(NST)
                               for x in ((n, 0, Bb), (n, 32, Cb))]
                    copy_dones = []
                    for i, (n, row, dstT) in enumerate(sel):
                        pb = p_bb0 if i % 2 == 0 else p_bb1
                        if i >= 2:
                            P.wait("pe", sDVE, copy_dones[i - 2])
                        P.op("pe", lambda E, nn=n, r=row, pp=pb: E.matmul(
                            pp[:], seye[r:r + 16, nn * 128:(nn + 1) * 128],
                            sbc[r:r + 16, :],
                            start=True, stop=True), incs=[(sPE, 1)])
                        P.wait("dve", sPE)
                        P.op("dve", lambda E, nn=n, pp=pb, dd=dstT: E.tensor_copy(
                            dd[:, nn * TC:(nn + 1) * TC], pp[:]))
                        copy_dones.append(sDVE.v)
                    bc_copy_done[0] = sDVE.v
                    sp_done = [0] * (NDT + 1)
                    dl_done = [0] * (NDT + 1)
                    delta_done = [0] * (NDT + 1)
                    sdx_done = [0] * (NDT + 1)
                    mul_done = [[0, 0] for _ in range(NDT + 1)]
                    exps_done = [[0, 0] for _ in range(NDT)]
                    scan_done = [[0, 0] for _ in range(NDT)]
                    xo_done = [0] * NDT
                    sq_done = [0] * NDT
                    y0_done = [0] * NDT

                    def emit_mm(d):
                        pdl = p_dl0 if d % 2 == 0 else p_dl1
                        if d >= 2:
                            P.wait("pe", sACT, sp_done[d - 2])
                        if d == 0:
                            P.wait("pe", sACT, bc_done)
                        P.op("pe", lambda E, dd=d, pp=pdl: E.matmul(
                            pp[:], sdtproj[:, dd * 128:(dd + 1) * 128], sdtin[:],
                            start=True, stop=True), incs=[(sPE, 1)])
                        dl_done[d] = sPE.v

                    def emit_softplus(d):
                        pdl = p_dl0 if d % 2 == 0 else p_dl1
                        s0 = (d % 2) * 2 * TC
                        P.wait("act", sPE, dl_done[d])
                        if d >= 2:
                            P.wait("act", sDVE, delta_done[d - 2])
                        P.op("act", lambda E, dd=d, pp=pdl, c=s0: E.activation(
                            ssp[:, c:c + TC], pp[:], AF.Abs,
                            bias=sdtpb[:, dd:dd + 1]))
                        P.fence("act")
                        P.op("act", lambda E, c=s0: E.activation(
                            ssp[:, c:c + TC], ssp[:, c:c + TC], AF.Exp,
                            scale=-1.0))
                        P.fence("act")
                        P.op("act", lambda E, c=s0: E.activation(
                            ssp[:, c + TC:c + 2 * TC], ssp[:, c:c + TC], AF.Ln,
                            bias=1.0))
                        P.fence("act")
                        P.op("act", lambda E, dd=d, pp=pdl, c=s0: E.activation(
                            ssp[:, c:c + TC], pp[:], AF.Relu,
                            bias=sdtpb[:, dd:dd + 1]))
                        sp_done[d] = sACT.v

                    def emit_delta(d):
                        s0 = (d % 2) * 2 * TC
                        P.wait("dve", sACT, sp_done[d])
                        if d >= 1:
                            P.wait("dve", sPOOL, sdx_done[d - 1])
                            P.wait("dve", sACT, exps_done[d - 1][1])
                        P.op("dve", lambda E, c=s0: E.tensor_add(
                            sdelta[:], ssp[:, c:c + TC],
                            ssp[:, c + TC:c + 2 * TC]))
                        delta_done[d] = sDVE.v

                    def emit_exps(d, h):
                        P.wait("act", sDVE, delta_done[d])
                        P.wait("act", ta_free[h][0], ta_free[h][1])
                        for n in range(NH):
                            g = d * NST + h * NH + n
                            P.op("act", lambda E, nn=n, gg=g, hh=h: E.activation(
                                tap[hh][:, nn * TC:(nn + 1) * TC], sdelta[:],
                                AF.Exp, scale=sA[:, gg:gg + 1]))
                        exps_done[d][h] = sACT.v

                    def emit_muls(d, h):
                        if h == 0:
                            P.fence("pool")
                            P.wait("pool", sDVE, delta_done[d])
                            P.wait("pool", sACT, xcv_done)
                            P.op("pool", lambda E, dd=d: E.tensor_tensor(
                                out=sdx[:], in0=sdelta[:],
                                in1=xcv[:, dd * TC:(dd + 1) * TC], op=ALU.mult))
                            sdx_done[d] = sPOOL.v
                        P.fence("pool")
                        P.wait("pool", tb_free[h][0], tb_free[h][1])
                        P.wait("pool", sDVE, bc_copy_done[0])
                        for n in range(NH):
                            g = h * NH + n
                            P.op("pool", lambda E, nn=n, gg=g, hh=h:
                                 E.tensor_tensor(
                                     out=tbp[hh][:, nn * TC:(nn + 1) * TC],
                                     in0=sdx[:],
                                     in1=Bb[:, gg * TC:(gg + 1) * TC],
                                     op=ALU.mult))
                        mul_done[d][h] = sPOOL.v

                    def emit_scan(d, h):
                        cb = d * NST + h * NH
                        P.wait("dve", sACT, exps_done[d][h])
                        P.wait("dve", sPOOL, mul_done[d][h])
                        P.fence("dve")
                        P.op("dve", lambda E, hh=h, c=cb: E.tensor_tensor(
                            out=sinj[:], in0=ta3[hh][:, :, 0],
                            in1=carry[:, c:c + NH], op=ALU.mult))
                        P.fence("dve")
                        P.op("dve", lambda E, hh=h: E.tensor_tensor(
                            out=tb3[hh][:, :, 0], in0=tb3[hh][:, :, 0],
                            in1=sinj[:], op=ALU.add))
                        P.op("dve", lambda E, hh=h: E.tensor_scalar_mul(
                            ta3[hh][:, :, 0], ta3[hh][:, :, 0], 0.0))
                        P.fence("dve")
                        P.op("dve", lambda E, hh=h: E.tensor_tensor_scan(
                            out=tbp[hh][:], data0=tap[hh][:], data1=tbp[hh][:],
                            initial=scv[:, C_ZERO:C_ZERO + 1],
                            op0=ALU.mult, op1=ALU.add))
                        scan_done[d][h] = sDVE.v
                        P.fence("dve")
                        P.op("dve", lambda E, hh=h, c=cb: E.tensor_copy(
                            carry[:, c:c + NH], tb3[hh][:, :, TC - 1]))

                    def emit_pool_y(d):
                        P.fence("pool")
                        P.wait("pool", sDVE, scan_done[d][0])
                        P.op("pool", lambda E: E.tensor_tensor(
                            out=tap[0][:], in0=tbp[0][:],
                            in1=Cb[:, 0:NH * TC], op=ALU.mult))
                        tb_free[0] = (sPOOL, sPOOL.v)
                        for w in (4 * TC, 2 * TC, TC):
                            P.fence("pool")
                            P.op("pool", lambda E, ww=w: E.tensor_tensor(
                                out=tap[0][:, 0:ww], in0=tap[0][:, 0:ww],
                                in1=tap[0][:, ww:2 * ww], op=ALU.add))
                        P.fence("pool")
                        if d >= 1:
                            P.wait("pool", sDVE, xo_done[d - 1])
                        P.op("pool", lambda E: E.tensor_copy(
                            y0[:], tap[0][:, 0:TC]))
                        y0_done[d] = sPOOL.v
                        ta_free[0] = (sPOOL, y0_done[d])

                    def emit_xo(d):
                        o = d * XOS + PADW + base_t
                        P.fence("dve")
                        P.op("dve", lambda E: E.tensor_mul(
                            tap[1][:], tbp[1][:], Cb[:, NH * TC:NST * TC]))
                        tb_free[1] = (sDVE, sDVE.v)
                        for w in (4 * TC, 2 * TC, TC):
                            P.fence("dve")
                            P.op("dve", lambda E, ww=w: E.tensor_add(
                                tap[1][:, 0:ww], tap[1][:, 0:ww],
                                tap[1][:, ww:2 * ww]))
                        if k < NCH - 1:
                            P.fence("dve")
                            P.op("dve", lambda E, oo=o, dd=d: E.tensor_copy(
                                hbuf[:, dd * 3:dd * 3 + 3],
                                xo[:, oo + TC - PADW:oo + TC]))
                        P.fence("dve")
                        P.op("dve", lambda E, oo=o, dd=d: E.scalar_tensor_tensor(
                            xo[:, oo:oo + TC], xo[:, oo:oo + TC],
                            sd1p[:, dd:dd + 1], tap[1][:, 0:TC],
                            ALU.mult, ALU.add))
                        ta_free[1] = (sDVE, sDVE.v)
                        P.fence("dve")
                        P.wait("dve", sPOOL, y0_done[d])
                        P.op("dve", lambda E, oo=o: E.tensor_add(
                            xo[:, oo:oo + TC], xo[:, oo:oo + TC], y0[:]))
                        xo_done[d] = sDVE.v

                    def emit_sq_stats(d):
                        o = d * XOS + PADW + base_t
                        P.wait("act", sDVE, xo_done[d])
                        P.wait("act", sPE, srb_free[0])
                        P.op("act", lambda E, oo=o: E.activation(
                            srq[:], xo[:, oo:oo + TC], AF.Square))
                        sq_done[d] = sACT.v
                        P.wait("pe", sDVE, xo_done[d])
                        P.wait("pe", sACT, sq_done[d])
                        P.op("pe", lambda E, dd=d, oo=o: E.matmul(
                            p_st[0:1, 0:TC], scv[:, C_ONE:C_ONE + 1],
                            xo[:, oo:oo + TC],
                            start=(dd == 0), stop=(dd == NDT - 1)))
                        P.op("pe", lambda E, dd=d: E.matmul(
                            p_st[0:1, TC:2 * TC], scv[:, C_ONE:C_ONE + 1],
                            srq[:], start=False, stop=(dd == NDT - 1)),
                            incs=[(sPE, 1)])
                        srb_free[0] = sPE.v

                    emit_mm(0)
                    emit_softplus(0)
                    emit_delta(0)
                    emit_muls(0, 0)
                    emit_muls(0, 1)
                    if NDT > 1:
                        emit_mm(1)
                    for d in range(NDT):
                        emit_exps(d, 0)
                        if not rc and d >= 1:
                            emit_sq_stats(d - 1)
                        if d + 1 < NDT:
                            emit_softplus(d + 1)
                        emit_exps(d, 1)
                        if d + 1 < NDT:
                            emit_delta(d + 1)
                        emit_scan(d, 0)
                        if rc:
                            ta_free[0] = (sDVE, scan_done[d][0])
                            tb_free[0] = (sDVE, sDVE.v)
                        else:
                            emit_pool_y(d)
                        if d + 1 < NDT:
                            emit_muls(d + 1, 0)
                        if d + 2 < NDT:
                            emit_mm(d + 2)
                        emit_scan(d, 1)
                        if rc:
                            ta_free[1] = (sDVE, scan_done[d][1])
                            tb_free[1] = (sDVE, sDVE.v)
                        else:
                            emit_xo(d)
                        if d + 1 < NDT:
                            emit_muls(d + 1, 1)
                    if not rc:
                        emit_sq_stats(NDT - 1)
                    if not rc:
                        P.wait("dve", sPE)
                        P.op("dve", lambda E: E.tensor_scalar_mul(
                            sst[0:1, 0:TC], p_st[0:1, 0:TC], 1.0 / D))
                        P.op("dve", lambda E: E.tensor_scalar_mul(
                            sst[0:1, TC:2 * TC], p_st[0:1, TC:2 * TC], 1.0 / D))
                        P.fence("dve")
                        P.op("dve", lambda E: E.tensor_mul(
                            sst[0:1, 2 * TC:3 * TC], sst[0:1, 0:TC],
                            sst[0:1, 0:TC]))
                        P.fence("dve")
                        P.op("dve", lambda E: E.tensor_sub(
                            sst[0:1, TC:2 * TC], sst[0:1, TC:2 * TC],
                            sst[0:1, 2 * TC:3 * TC]))
                        P.wait("act", sDVE)
                        P.op("act", lambda E: E.activation(
                            sst[0:1, 2 * TC:3 * TC], sst[0:1, TC:2 * TC], AF.Ln,
                            bias=scv[0:1, C_EPS:C_EPS + 1]))
                        P.fence("act")
                        P.op("act", lambda E: E.activation(
                            sst[0:1, TC:2 * TC], sst[0:1, 2 * TC:3 * TC], AF.Exp,
                            scale=scv[0:1, C_NEGHALF:C_NEGHALF + 1]))
                        P.wait("pe", sACT)
                        P.op("pe", lambda E: E.matmul(
                            p_bcst[:, 0:TC], sones1[:], sst[0:1, 0:TC],
                            start=True, stop=True))
                        P.op("pe", lambda E: E.matmul(
                            p_bcst[:, TC:2 * TC], sones1[:], sst[0:1, TC:2 * TC],
                            start=True, stop=True), incs=[(sPE, 1)])
                        P.wait("dve", sPE)
                        for dti in range(NDT):
                            xoff = dti * XOS + PADW + base_t
                            P.op("dve", lambda E, o=xoff: E.tensor_sub(
                                xo[:, o:o + TC], xo[:, o:o + TC], p_bcst[:, 0:TC]))
                            P.fence("dve")
                            P.op("dve", lambda E, o=xoff: E.tensor_mul(
                                xo[:, o:o + TC], xo[:, o:o + TC],
                                p_bcst[:, TC:2 * TC]))
                            mul_done = sDVE.v
                            P.wait("pool", sDVE, mul_done)
                            P.op("pool", lambda E, o=xoff, d=dti: E.tensor_copy(
                                xb[:, d * MC + base_t:d * MC + base_t + TC],
                                xo[:, o:o + TC]))
                    barc["pe"] = sPE.v
                    barc["act"] = sACT.v
                    barc["dve"] = sDVE.v
                    barc["pool"] = sPOOL.v

                for g in range(4):
                    P.op("sp", lambda E, gg=g: E.dma_start(
                        out=wbufv[:, 0:NDT * WTS].rearrange(
                            "p (dt t) -> p dt t", dt=NDT)
                        [:, gg * 4:(gg + 1) * 4, :],
                        in_=xdt_pre.rearrange("(dt p) t -> p dt t", p=128)
                        [:, gg * 4:(gg + 1) * 4, 0:WTS]),
                        incs=[(sRC, 16)])
                for kk in range(NCH):
                    mamba_chunk(True, kk)
                P.wait("dve", sIN, setup_in)
                P.fence("dve")
                P.op("dve", lambda E: E.tensor_scalar_mul(
                    carry[:], carry[:], srcm[:, 0:1]))
                P.fence("dve")
                for kk in range(NCH):
                    mamba_chunk(False, kk)

                if debug:
                    bwait("sp")
                    for dti in range(NDT):
                        P.op("sp", lambda E, d=dti: E.dma_start(
                            out=dbg_xin[:, d * MC:(d + 1) * MC],
                            in_=xo[:, d * XOS + PADW:d * XOS + PADW + MC]),
                            incs=[(sOUT, 16)])

            acc = sbuf("acc", [128, NTT * D], F32)
            xloq = sbuf("xloq", [128, 4 * 128], BF16)
            sbt = sbuf("sbt", [128, 512], BF16)
            with ExitStack() as tstack:
                p_lg = tstack.enter_context(nc.psum_tensor("p_lg", [128, 8], F32))
                p_tr = tstack.enter_context(
                    nc.psum_tensor("p_tr", [128, 1024], BF16))

                bwait("pe")
                bwait("act")
                bwait("dve")
                for tt in range(NTT):
                    for dti in range(NDT):
                        P.op("pe", lambda E, d=dti, t=tt: E.matmul(
                            p_lg[:],
                            xo[:, d * XOS + PADW + t * 128:
                               d * XOS + PADW + (t + 1) * 128],
                            swg[:, d * 8:(d + 1) * 8],
                            start=(d == 0), stop=(d == NDT - 1)),
                            incs=[(sPE, 1)] if dti == NDT - 1 else ())

                    def r0(c):
                        return lsc[:, c * 8:(c + 1) * 8]
                    P.wait("dve", sPE)
                    P.op("dve", lambda E: E.reduce_max(lsc[:, 64:65], p_lg[:], AX.X))
                    P.fence("dve")
                    P.op("dve", lambda E: E.tensor_scalar_mul(
                        lsc[:, 65:66], lsc[:, 64:65], -1.0))
                    P.op("dve", lambda E: E.tensor_copy(r0(0), p_lg[:]))
                    P.wait("act", sDVE)
                    P.op("act", lambda E: E.activation(
                        r0(1), r0(0), AF.Exp, bias=lsc[:, 65:66]))
                    P.wait("dve", sACT)
                    P.op("dve", lambda E: E.reduce_max(lsc[:, 64:65], r0(1), AX.X))
                    P.fence("dve")
                    P.op("dve", lambda E: E.tensor_single_scalar(
                        r0(2), r0(1), lsc[:, 64:65], ALU.is_equal))
                    P.fence("dve")
                    P.op("dve", lambda E: E.tensor_scalar(
                        r0(3), r0(2), -1.0, 1.0, ALU.mult, ALU.add))
                    P.fence("dve")
                    P.op("dve", lambda E: E.tensor_mul(r0(3), r0(3), r0(1)))
                    P.fence("dve")
                    P.op("dve", lambda E: E.reduce_max(lsc[:, 66:67], r0(3), AX.X))
                    P.fence("dve")
                    P.op("dve", lambda E: E.tensor_single_scalar(
                        r0(4), r0(3), lsc[:, 66:67], ALU.is_equal))
                    P.fence("dve")
                    P.op("dve", lambda E: E.tensor_add(r0(2), r0(2), r0(4)))
                    P.op("dve", lambda E: E.tensor_add(
                        lsc[:, 67:68], lsc[:, 64:65], lsc[:, 66:67]))
                    P.fence("dve")
                    P.op("dve", lambda E: E.reciprocal(lsc[:, 68:69], lsc[:, 67:68]))
                    P.fence("dve")
                    P.op("dve", lambda E: E.tensor_mul(r0(2), r0(2), r0(1)))
                    P.fence("dve")
                    P.op("dve", lambda E, t=tt: E.tensor_scalar_mul(
                        comb[:, t * 8:(t + 1) * 8], r0(2), lsc[:, 68:69]))
                    P.wait("pe", sDVE)

                if debug:
                    P.wait("sp", sDVE)
                    P.op("sp", lambda E: E.dma_start(out=dbg_comb, in_=comb[:]),
                         incs=[(sOUT, 16)])

                for tt in range(NTT):
                    for dq in range(4):
                        P.wait("dve", sPE)
                        for i in range(4):
                            dti = dq * 4 + i
                            xoff = dti * XOS + PADW + tt * 128
                            P.op("dve", lambda E, o=xoff, d=dti, ii=i, t=tt:
                                 E.tensor_sub(
                                     xloq[:, ii * 128:(ii + 1) * 128],
                                     xo[:, o:o + 128],
                                     xb[:, d * MC + t * 128:d * MC + t * 128 + 128]))
                        P.wait("pe", sDVE)
                        for i in range(4):
                            dti = dq * 4 + i
                            P.op("pe", lambda E, d=dti, ii=i, t=tt: E.transpose(
                                p_tr[:, ii * 128:(ii + 1) * 128],
                                xb[:, d * MC + t * 128:d * MC + t * 128 + 128],
                                sident[:]))
                            P.op("pe", lambda E, ii=i: E.transpose(
                                p_tr[:, 512 + ii * 128:512 + (ii + 1) * 128],
                                xloq[:, ii * 128:(ii + 1) * 128], sident[:]),
                                incs=[(sPE, 1)] if i == 3 else ())
                        P.wait("dve", sPE)
                        P.op("dve", lambda E: E.tensor_copy(sbt[:], p_tr[:, 0:512]))
                        P.fence("dve")
                        P.op("dve", lambda E, t=tt, q=dq: E.tensor_add(
                            acc[:, t * D + q * 512:t * D + (q + 1) * 512],
                            sbt[:], p_tr[:, 512:1024]))
                seed_dve = sDVE.v

        H = sbuf("H", [128, NF * MC], BF16, side="right")
        wgb = [sbuf("wgb0", [128, D], BF16), sbuf("wgb1", [128, D], BF16)]
        wub = [sbuf("wub0", [128, D], BF16), sbuf("wub1", [128, D], BF16)]
        wdb = [sbuf("wdb0", [128, D], BF16), sbuf("wdb1", [128, D], BF16)]
        tsil = sbuf("tsil", [128, MC], F32)
        sq16 = sbuf("sq16", [128, D], BF16)
        pG = en(nc.psum_tensor("pG", [128, MC], F32))
        pU = en(nc.psum_tensor("pU", [128, MC], F32))
        pdA = en(nc.psum_tensor("pdA", [128, MC], F32))
        pdB = en(nc.psum_tensor("pdB", [128, MC], F32))

        def pd_slice(blk, tt):
            grpA = (blk % 2 == 0)
            if tt < 4:
                base = pG if grpA else pdA
            else:
                base = pU if grpA else pdB
            c = (tt % 4) * 256
            return base[:, c:c + 256]

        ngl = [0]
        ndl = [0]
        silu_cap = []
        hm_cap = []
        ev_cap = []

        for e in range(n_experts):
            for f in range(NF):
                fgi = e * NF + f
                par = f % 2
                if ngl[0] == 0:
                    P.wait("pool", sDVE, seed_dve)
                if ngl[0] >= 2:
                    P.wait("pool", sGf, ngl[0] - 1)
                    P.wait("pool", sUf, ngl[0] - 1)
                row = (e * NF + f) * 128
                P.op("pool", lambda E, r=row, p=par: E.dma_start(
                    out=wgb[p][:], in_=gate_w[r:r + 128, :]),
                    incs=[(sWg[par], 16)])
                P.op("pool", lambda E, r=row, p=par: E.dma_start(
                    out=wub[p][:], in_=up_w[r:r + 128, :]),
                    incs=[(sWu[par], 16)])
                ngl[0] += 1
                P.wait("pe", sWg[par], sWg[par].v)
                if fgi > 0:
                    P.wait("pe", sACT, silu_cap[fgi - 1])
                if e == 0 and f == 0:
                    P.wait("pe", sDVE, seed_dve)
                if f == 0 and e > 0:
                    P.wait("pe", sDVE, ev_cap[-1])
                for dti in range(NDT):
                    for h in range(2):
                        P.op("pe", lambda E, d=dti, hh=h, p=par: E.matmul(
                            pG[:, hh * 512:(hh + 1) * 512],
                            wgb[p][:, d * 128:(d + 1) * 128],
                            xb[:, d * MC + hh * 512:d * MC + (hh + 1) * 512],
                            start=(d == 0), stop=(d == NDT - 1)),
                            incs=[(sGf, 1)] if (dti == NDT - 1 and h == 1) else ())
                P.wait("pe", sWu[par], sWu[par].v)
                if fgi > 0:
                    P.wait("pe", sDVE, hm_cap[fgi - 1])
                for dti in range(NDT):
                    for h in range(2):
                        P.op("pe", lambda E, d=dti, hh=h, p=par: E.matmul(
                            pU[:, hh * 512:(hh + 1) * 512],
                            wub[p][:, d * 128:(d + 1) * 128],
                            xb[:, d * MC + hh * 512:d * MC + (hh + 1) * 512],
                            start=(d == 0), stop=(d == NDT - 1)),
                            incs=[(sUf, 1)] if (dti == NDT - 1 and h == 1) else ())
                P.wait("act", sGf, fgi + 1)
                if fgi > 0:
                    P.wait("act", sDVE, hm_cap[fgi - 1])
                P.op("act", lambda E: E.activation(tsil[:], pG[:], AF.Silu))
                silu_cap.append(sACT.v)
                P.wait("dve", sACT, silu_cap[fgi])
                P.wait("dve", sUf, sUf.v)
                P.op("dve", lambda E, ff=f: E.tensor_tensor(
                    H[:, ff * MC:(ff + 1) * MC], tsil[:], pU[:], ALU.mult))
                hm_cap.append(sDVE.v)
            P.wait("pe", sDVE, hm_cap[-1])
            for blk in range(8):
                for fq in range(4):
                    if ndl[0] >= 2:
                        P.wait("pool", sDf, ndl[0] - 1)
                    rowd = (e * NF + blk * 4 + fq) * 128
                    pard = ndl[0] % 2
                    P.op("pool", lambda E, r=rowd, p=pard: E.dma_start(
                        out=wdb[p][:], in_=down_w[r:r + 128, :]),
                        incs=[(sWd[pard], 16)])
                    ndl[0] += 1
                    P.wait("pe", sWd[pard], sWd[pard].v)
                    if fq == 0 and len(ev_cap) >= 2:
                        P.wait("pe", sDVE, ev_cap[-2])
                    for f8 in range(8):
                        f = fq * 8 + f8
                        for tt in range(NTT):
                            last_piece = (f8 == 7 and tt == NTT - 1)
                            P.op("pe", lambda E, ff=f, t=tt, b=blk, p=pard, q=fq,
                                 f8_=f8: E.matmul(
                                     pd_slice(b, t),
                                     H[:, ff * MC + t * 128:ff * MC + (t + 1) * 128],
                                     wdb[p][:, f8_ * 256:(f8_ + 1) * 256],
                                     start=(q == 0 and f8_ == 0 and t % 2 == 0),
                                     stop=(q == 3 and f8_ == 7)),
                                 incs=[(sDf, 1)] if last_piece else ())
                P.wait("dve", sDf, ndl[0])
                for tt in range(NTT):
                    P.fence("dve")
                    if e == 0:
                        P.op("dve", lambda E, b=blk, t=tt: E.scalar_tensor_tensor(
                            acc[:, t * D + b * 256:t * D + (b + 1) * 256],
                            pd_slice(b, t), 1.0,
                            acc[:, t * D + b * 256:t * D + (b + 1) * 256],
                            ALU.mult, ALU.add))
                    else:
                        P.op("dve", lambda E, b=blk, t=tt, ee=e:
                             E.scalar_tensor_tensor(
                                 acc[:, t * D + b * 256:t * D + (b + 1) * 256],
                                 pd_slice(b, t),
                                 comb[:, t * 8 + ee - 1:t * 8 + ee],
                                 acc[:, t * D + b * 256:t * D + (b + 1) * 256],
                                 ALU.mult, ALU.add))
                ev_cap.append(sDVE.v)

        P.wait("dve", sDVE, ev_cap[-1])
        for tt in range(NTT):
            a_tt = acc[:, tt * D:(tt + 1) * D]
            P.fence("dve")
            P.op("dve", lambda E, a=a_tt: E.reduce_sum(lsc[:, 64:65], a, AX.X))
            P.fence("dve")
            P.op("dve", lambda E: E.tensor_scalar_mul(
                lsc[:, 64:65], lsc[:, 64:65], 1.0 / D))
            P.fence("dve")
            P.op("dve", lambda E, a=a_tt: E.tensor_single_scalar(
                a, a, lsc[:, 64:65], ALU.subtract))
            P.wait("act", sDVE)
            P.op("act", lambda E, a=a_tt: E.activation(sq16[:], a, AF.Square))
            P.wait("dve", sACT)
            P.op("dve", lambda E: E.reduce_sum(lsc[:, 65:66], sq16[:], AX.X))
            P.fence("dve")
            P.op("dve", lambda E: E.tensor_scalar_mul(
                lsc[:, 65:66], lsc[:, 65:66], 1.0 / D))
            P.wait("act", sDVE)
            P.op("act", lambda E: E.activation(
                lsc[:, 66:67], lsc[:, 65:66], AF.Ln,
                bias=scv[:, C_EPS:C_EPS + 1]))
            P.fence("act")
            P.op("act", lambda E: E.activation(
                lsc[:, 67:68], lsc[:, 66:67], AF.Exp,
                scale=scv[:, C_NEGHALF:C_NEGHALF + 1]))
            P.wait("dve", sACT)
            P.op("dve", lambda E, a=a_tt: E.tensor_scalar_mul(
                a, a, lsc[:, 67:68]))
            P.wait("sp", sDVE)
            P.op("sp", lambda E, t=tt, a=a_tt: E.dma_start(
                out=out_hbm[t * 128:(t + 1) * 128, :], in_=a),
                incs=[(sOUT, 16)])
        fin = sOUT.v
        P.q["sp"].append(lambda E, t=fin: E.wait_ge(sOUT.h, t))

        with nc.Block() as block:
            @block.tensor
            def _(E):
                for fn in P.q["pe"]:
                    fn(E)

            @block.scalar
            def _(E):
                for fn in P.q["act"]:
                    fn(E)

            @block.vector
            def _(E):
                for fn in P.q["dve"]:
                    fn(E)

            @block.gpsimd
            def _(E):
                for fn in P.q["pool"]:
                    fn(E)

            @block.sync
            def _(E):
                for fn in P.q["sp"]:
                    fn(E)

    return nc



def _prep_shared(i):
    f32 = np.float32
    import ml_dtypes
    sh = {}
    sh["cvec"] = np.tile(np.array([[EPS, -0.5, 0.0, 1.0]], dtype=f32), (128, 1))
    cw = np.ascontiguousarray(np.asarray(i["L0_attn_conv_w"], f32)[:, 0, :])
    sh["convw_t"] = np.ascontiguousarray(
        cw.reshape(NDT, 128, 4).transpose(1, 0, 2).reshape(128, NDT * 4))
    for name, vec in [("convb_t", i["L0_attn_conv_b"]),
                      ("dtprojb_t", i["L0_attn_dtproj_b"]),
                      ("d1p_t", 1.0 + np.asarray(i["L0_attn_D_param"], f32))]:
        sh[name] = np.ascontiguousarray(
            np.asarray(vec, dtype=f32).reshape(NDT, 128).T)
    sh["alog_t"] = np.ascontiguousarray(
        np.asarray(i["L0_attn_A_log"], f32).reshape(NDT, 128, NST)
        .transpose(1, 0, 2).reshape(128, NDT * NST))
    xp = np.asarray(i["L0_attn_xproj_w"], f32)
    sh["xproj_t"] = np.ascontiguousarray(
        xp.T.reshape(NDT, 128, 160).transpose(1, 0, 2).reshape(128, NDT * 160))
    sh["dtproj_t"] = np.ascontiguousarray(np.asarray(i["L0_attn_dtproj_w"], f32).T)
    wg = np.asarray(i["L0_ffn_wg"], f32)
    sh["wg_t"] = np.ascontiguousarray(
        wg.T.reshape(NDT, 128, 8).transpose(1, 0, 2).reshape(128, NDT * 8))
    eye = np.zeros((48, NST * 128), dtype=f32)
    for n in range(NST):
        eye[n, n * 128:(n + 1) * 128] = 1.0
        eye[32 + n, n * 128:(n + 1) * 128] = 1.0
    sh["eye16"] = eye.astype(ml_dtypes.bfloat16)
    sh["ones1_128"] = np.ones((1, 128), dtype=f32)
    sh["ones128_bf"] = np.ones((128, 1), dtype=ml_dtypes.bfloat16)
    sh["ident_bf"] = np.eye(128, dtype=f32).astype(ml_dtypes.bfloat16)

    def tile_gu(mats):
        out = np.empty((NE * NF * 128, D), dtype=f32)
        for e, W in enumerate(mats):
            t = W.reshape(NF, 128, NDT, 128).transpose(0, 3, 2, 1)
            out[e * NF * 128:(e + 1) * NF * 128] = t.reshape(NF * 128, D)
        return out

    gmats = [np.asarray(i["L0_ffn_sh_gate"], f32)] + \
            [np.asarray(i["L0_ffn_ex_gate"][e], f32) for e in range(8)]
    umats = [np.asarray(i["L0_ffn_sh_up"], f32)] + \
            [np.asarray(i["L0_ffn_ex_up"][e], f32) for e in range(8)]
    sh["gate_w"] = tile_gu(gmats)
    sh["up_w"] = tile_gu(umats)

    def tile_down(mats):
        out = np.empty((NE * NF * 128, D), dtype=f32)
        for e, Wd in enumerate(mats):
            t = Wd.reshape(8, 256, 4, 8, 128)
            t = t.transpose(0, 2, 4, 3, 1)
            out[e * NF * 128:(e + 1) * NF * 128] = t.reshape(NF * 128, D)
        return out

    dmats = [np.asarray(i["L0_ffn_sh_down"], f32)] + \
            [np.asarray(i["L0_ffn_ex_down"][e], f32) for e in range(8)]
    sh["down_w"] = tile_down(dmats)
    return sh


def _prep_core(x, b, h):
    f32 = np.float32
    xt = np.asarray(x[b], f32).T
    own = np.zeros((D, XOS), dtype=f32)
    own[:, PADW:] = xt[:, h * MC:(h + 1) * MC]
    if h == 1:
        own[:, 0:PADW] = xt[:, MC - PADW:MC]
    pre = np.zeros((D, XOS), dtype=f32)
    if h == 1:
        pre[:, PADW:] = xt[:, 0:MC]
    return np.ascontiguousarray(own), np.ascontiguousarray(pre)


def kernel(**inputs):
    x = np.asarray(inputs["L0_attn_x"], np.float32)
    sh = _prep_shared(inputs)
    nc = build_program(debug=False)
    in_maps = []
    for core in range(8):
        b, h = core // 2, core % 2
        own, pre = _prep_core(x, b, h)
        m = dict(sh)
        m["xdt_own"] = own
        m["xdt_pre"] = pre
        m["rcmask"] = np.full((128, 1), 1.0 if h == 1 else 0.0, np.float32)
        in_maps.append(m)
    res = run_bass_kernel_spmd(nc, in_maps, core_ids=list(range(8)))
    out = np.empty((4, 2048, D), dtype=np.float32)
    for core in range(8):
        b, h = core // 2, core % 2
        out[b, h * MC:(h + 1) * MC, :] = res.results[core]["out"]
    return out


# version 07
# speedup vs baseline: 1.0755x; 1.0414x over previous
import numpy as np
from contextlib import ExitStack
import concourse.bass as bass
import concourse.mybir as mybir
from concourse.bass_utils import run_bass_kernel_spmd

F32 = mybir.dt.float32
BF16 = mybir.dt.bfloat16
AF = mybir.ActivationFunctionType
ALU = mybir.AluOpType
AX = mybir.AxisListType

D = 2048
NDT = 16
MC = 1024
TC = 256
NCH = MC // TC
NST = 16
DTR = 128
FF = 4096
NF = 32
NE = 9
PADW = 3
XOS = PADW + MC
WTS = TC + PADW
NTT = 8
EPS = 1e-5

C_EPS = 0
C_NEGHALF = 1
C_ZERO = 2
C_ONE = 3
C_NCONST = 4
NH = NST // 2


class S:
    def __init__(self, stack, nc, name):
        self.h = stack.enter_context(nc.semaphore(name))
        self.v = 0


class Prog:

    ENGS = ("pe", "act", "dve", "pool", "sp")

    def __init__(self):
        self.q = {e: [] for e in self.ENGS}
        self.auto = {}

    def wait(self, eng, s, at=None):
        tgt = s.v if at is None else at
        if tgt > 0:
            self.q[eng].append(lambda E, h=s.h, t=tgt: E.wait_ge(h, t))

    def op(self, eng, fn, incs=(), noauto=False):
        a = self.auto.get(eng)
        if a is not None and not incs and not noauto:
            incs = ((a, 1),)
        if incs:
            def wrapped(E, f=fn, pairs=tuple((s.h, n) for s, n in incs)):
                ins = f(E)
                for h, n in pairs:
                    ins.then_inc(h, n)
                return ins
            self.q[eng].append(wrapped)
            for s, n in incs:
                s.v += n
        else:
            self.q[eng].append(fn)

    def fence(self, eng):
        self.wait(eng, self.auto[eng])


def build_program(debug=False, n_experts=NE):
    nc = bass.Bass("TRN2", target_bir_lowering=False, debug=False)
    dtr = nc.dram_tensor

    xdt_own = dtr("xdt_own", [D, XOS], F32, kind="ExternalInput").ap()
    xdt_pre = dtr("xdt_pre", [D, XOS], F32, kind="ExternalInput").ap()
    rcmask = dtr("rcmask", [128, 1], F32, kind="ExternalInput").ap()
    cvec = dtr("cvec", [128, C_NCONST], F32, kind="ExternalInput").ap()
    convw_t = dtr("convw_t", [128, NDT * 4], F32, kind="ExternalInput").ap()
    convb_t = dtr("convb_t", [128, NDT], F32, kind="ExternalInput").ap()
    dtprojb_t = dtr("dtprojb_t", [128, NDT], F32, kind="ExternalInput").ap()
    d1p_t = dtr("d1p_t", [128, NDT], F32, kind="ExternalInput").ap()
    alog_t = dtr("alog_t", [128, NDT * NST], F32, kind="ExternalInput").ap()
    xproj_t = dtr("xproj_t", [128, NDT * 160], F32, kind="ExternalInput").ap()
    dtproj_t = dtr("dtproj_t", [DTR, D], F32, kind="ExternalInput").ap()
    wg_t = dtr("wg_t", [128, NDT * 8], F32, kind="ExternalInput").ap()
    eye16 = dtr("eye16", [48, NST * 128], BF16, kind="ExternalInput").ap()
    ones1_128 = dtr("ones1_128", [1, 128], F32, kind="ExternalInput").ap()
    ones128_bf = dtr("ones128_bf", [128, 1], BF16, kind="ExternalInput").ap()
    ident_bf = dtr("ident_bf", [128, 128], BF16, kind="ExternalInput").ap()
    gate_w = dtr("gate_w", [NE * NF * 128, D], F32, kind="ExternalInput").ap()
    up_w = dtr("up_w", [NE * NF * 128, D], F32, kind="ExternalInput").ap()
    down_w = dtr("down_w", [NE * NF * 128, D], F32, kind="ExternalInput").ap()
    out_hbm = dtr("out", [MC, D], F32, kind="ExternalOutput").ap()
    if debug:
        dbg_xin = dtr("dbg_xin", [128, NDT * MC], F32, kind="ExternalOutput").ap()
        dbg_comb = dtr("dbg_comb", [128, NTT * 8], F32, kind="ExternalOutput").ap()

    P = Prog()

    with ExitStack() as stack:
        en = stack.enter_context

        def sbuf(name, shape, dtype, side="left"):
            return en(nc.sbuf_tensor(name, shape, dtype, side=side))

        sIN = S(stack, nc, "sIN")
        sRC = S(stack, nc, "sRC")
        sOUT = S(stack, nc, "sOUT")
        sW = S(stack, nc, "sW")
        sWg = [S(stack, nc, "sWg0"), S(stack, nc, "sWg1")]
        sWu = [S(stack, nc, "sWu0"), S(stack, nc, "sWu1")]
        sWd = [S(stack, nc, "sWd0"), S(stack, nc, "sWd1")]
        sPE = S(stack, nc, "sPE")
        sACT = S(stack, nc, "sACT")
        sDVE = S(stack, nc, "sDVE")
        sGf = S(stack, nc, "sGf")
        sUf = S(stack, nc, "sUf")
        sDf = S(stack, nc, "sDf")
        sPOOL = S(stack, nc, "sPOOL")
        P.auto = {"dve": sDVE, "act": sACT, "pool": sPOOL}

        scv = sbuf("scv", [128, C_NCONST], F32)
        nc.const_aps.register(F32, EPS, scv[:, C_EPS:C_EPS + 1])
        nc.const_aps.register(F32, -0.5, scv[:, C_NEGHALF:C_NEGHALF + 1])
        swg = sbuf("swg", [128, NDT * 8], F32)
        sident = sbuf("sident", [128, 128], BF16)
        xb = sbuf("xb", [128, NDT * MC], BF16)
        carry = sbuf("carry", [128, NDT * NST], F32)
        comb = sbuf("comb", [128, NTT * 8], F32)
        lsc = sbuf("lsc", [128, 80], F32)

        barc = {"pe": 0, "act": 0, "dve": 0, "pool": 0}

        def bwait(eng):
            P.wait(eng, sPE, barc["pe"])
            P.wait(eng, sACT, barc["act"])
            P.wait(eng, sDVE, barc["dve"])
            P.wait(eng, sPOOL, barc["pool"])

        with ExitStack() as xo_stack:
            xo = xo_stack.enter_context(
                nc.sbuf_tensor("xo", [128, NDT * XOS], F32, side="right"))
            with ExitStack() as mstack:
                men = mstack.enter_context

                def msbuf(name, shape, dtype):
                    return men(nc.sbuf_tensor(name, shape, dtype, side="right"))

                sconvw = msbuf("sconvw", [128, NDT * 4], F32)
                sconvb = msbuf("sconvb", [128, NDT], F32)
                sdtpb = msbuf("sdtpb", [128, NDT], F32)
                sd1p = msbuf("sd1p", [128, NDT], F32)
                sA = msbuf("sA", [128, NDT * NST], F32)
                sxproj = msbuf("sxproj", [128, NDT * 160], BF16)
                sdtproj = msbuf("sdtproj", [DTR, D], BF16)
                seye = msbuf("seye", [48, NST * 128], BF16)
                sones1 = msbuf("sones1", [1, 128], F32)
                sonesb = msbuf("sonesb", [128, 1], BF16)
                sdelta = msbuf("sdelta", [128, 2 * TC], F32)
                sdx = msbuf("sdx", [128, TC], F32)
                ssp = msbuf("ssp", [128, 4 * TC], F32)
                srq = msbuf("srq", [128, TC], F32)
                sst = msbuf("sst", [1, 3 * TC], F32)
                sdtin = msbuf("sdtin", [128, TC], BF16)
                sbc = msbuf("sbc", [64, TC], BF16)
                srcm = msbuf("srcm", [128, 1], F32)
                sinj = msbuf("sinj", [128, NH], F32)
                hbuf = msbuf("hbuf", [128, NDT * PADW], F32)
                y0 = msbuf("y0", [128, TC], F32)
                Bb = msbuf("Bb", [128, NST * TC], BF16)
                Cb = msbuf("Cb", [128, NST * TC], BF16)
                tap = [msbuf("ta0", [128, NH * TC], F32),
                       msbuf("ta1", [128, NH * TC], F32)]
                tbp = [msbuf("tb0", [128, NH * TC], F32),
                       msbuf("tb1", [128, NH * TC], F32)]
                xcv = msbuf("xcv", [128, NDT * TC], BF16)
                wbufv = xb[:].bitcast(F32)

                p_proj = men(nc.psum_tensor("p_proj", [128, TC], F32))
                p_bc = men(nc.psum_tensor("p_bc", [64, TC], F32))
                p_dl0 = men(nc.psum_tensor("p_dl0", [128, TC], F32))
                p_dl1 = men(nc.psum_tensor("p_dl1", [128, TC], F32))
                p_bb0 = men(nc.psum_tensor("p_bb0", [128, TC], F32))
                p_bb1 = men(nc.psum_tensor("p_bb1", [128, TC], F32))
                p_st = men(nc.psum_tensor("p_st", [1, 2 * TC], F32))
                p_bcst = men(nc.psum_tensor("p_bcst", [128, 2 * TC], F32))

                for dst, src in [
                    (scv[:], cvec), (sconvw[:], convw_t), (sconvb[:], convb_t),
                    (sdtpb[:], dtprojb_t), (sd1p[:], d1p_t), (sA[:], alog_t),
                    (swg[:], wg_t), (seye[:], eye16), (sones1[:], ones1_128),
                    (sonesb[:], ones128_bf), (sident[:], ident_bf),
                    (srcm[:], rcmask),
                ]:
                    P.op("sp", lambda E, d=dst, s=src: E.dma_start(out=d, in_=s),
                         incs=[(sIN, 16)])
                P.op("pool", lambda E: E.dma_start(out=sxproj[:], in_=xproj_t),
                     incs=[(sW, 16)])
                P.op("pool", lambda E: E.dma_start(out=sdtproj[:], in_=dtproj_t),
                     incs=[(sW, 16)])
                P.op("sp", lambda E: E.dma_start(
                    out=xo[:].rearrange("p (dt t) -> p dt t", dt=NDT),
                    in_=xdt_own.rearrange("(dt p) t -> p dt t", p=128)),
                    incs=[(sIN, 16)])
                setup_in = sIN.v
                setup_w = sW.v

                P.wait("act", sIN, setup_in)
                P.op("act", lambda E: E.activation(sA[:], sA[:], AF.Exp))
                P.wait("dve", sACT)
                P.op("dve", lambda E: E.tensor_scalar_mul(sA[:], sA[:], -1.0))
                P.op("dve", lambda E: E.memset(carry[:], 0.0))
                amaterial = sDVE.v

                ta3 = [t[:].rearrange("p (n t) -> p n t", n=NH) for t in tap]
                tb3 = [t[:].rearrange("p (n t) -> p n t", n=NH) for t in tbp]
                ta_free = [(sDVE, 0), (sDVE, 0)]
                tb_free = [(sDVE, 0), (sDVE, 0)]
                srb_free = [0]
                bc_copy_done = [0]

                def mamba_chunk(rc, k):
                    base_t = k * TC
                    bwait("dve")
                    bwait("act")
                    bwait("pe")
                    bwait("pool")
                    if rc:
                        P.wait("dve", sRC)
                        P.wait("dve", sDVE, amaterial)
                    else:
                        P.wait("dve", sIN, setup_in)
                    for dti in range(NDT):
                        xc = ssp[:, 0:TC]
                        if dti > 0:
                            P.wait("dve", sACT)
                        if rc or k == 0:
                            src = wbufv if rc else xo[:]
                            b0 = dti * (WTS if rc else XOS) + PADW
                            P.op("dve", lambda E, s=src, b=b0, d=dti:
                                 E.tensor_scalar_mul(
                                     xc, s[:, b - 3:b - 3 + TC],
                                     sconvw[:, d * 4:d * 4 + 1]))
                            for j in range(1, 4):
                                P.fence("dve")
                                P.op("dve", lambda E, s=src, b=b0, d=dti, jj=j:
                                     E.scalar_tensor_tensor(
                                         xc, s[:, b - 3 + jj:b - 3 + jj + TC],
                                         sconvw[:, d * 4 + jj:d * 4 + jj + 1], xc,
                                         ALU.mult, ALU.add))
                        else:
                            o = dti * XOS + PADW + base_t
                            P.op("dve", lambda E, oo=o, d=dti:
                                 E.tensor_scalar_mul(
                                     xc[:, 3:TC], xo[:, oo:oo + TC - 3],
                                     sconvw[:, d * 4:d * 4 + 1]))
                            P.op("dve", lambda E, d=dti: E.tensor_scalar_mul(
                                xc[:, 0:3], hbuf[:, d * 3:d * 3 + 3],
                                sconvw[:, d * 4:d * 4 + 1]))
                            for j in range(1, 4):
                                P.fence("dve")
                                P.op("dve", lambda E, oo=o, d=dti, jj=j:
                                     E.scalar_tensor_tensor(
                                         xc[:, 3 - jj:TC],
                                         xo[:, oo:oo + TC - 3 + jj],
                                         sconvw[:, d * 4 + jj:d * 4 + jj + 1],
                                         xc[:, 3 - jj:TC], ALU.mult, ALU.add))
                                if j < 3:
                                    P.op("dve", lambda E, d=dti, jj=j:
                                         E.scalar_tensor_tensor(
                                             xc[:, 0:3 - jj],
                                             hbuf[:, d * 3 + jj:d * 3 + 3],
                                             sconvw[:, d * 4 + jj:d * 4 + jj + 1],
                                             xc[:, 0:3 - jj], ALU.mult, ALU.add))
                        P.wait("act", sDVE)
                        P.op("act", lambda E, d=dti: E.activation(
                            xcv[:, d * TC:(d + 1) * TC], xc, AF.Silu,
                            bias=sconvb[:, d:d + 1]))
                    conv_dve_done = sDVE.v
                    xcv_done = sACT.v
                    if rc and k + 1 < NCH:
                        P.wait("sp", sDVE, conv_dve_done)
                        for g in range(4):
                            P.op("sp", lambda E, k2=k + 1, gg=g: E.dma_start(
                                out=wbufv[:, 0:NDT * WTS].rearrange(
                                    "p (dt t) -> p dt t", dt=NDT)
                                [:, gg * 4:(gg + 1) * 4, :],
                                in_=xdt_pre.rearrange("(dt p) t -> p dt t", p=128)
                                [:, gg * 4:(gg + 1) * 4,
                                 k2 * TC:k2 * TC + WTS]),
                                incs=[(sRC, 16)])
                    P.wait("pe", sACT, xcv_done)
                    P.wait("pe", sW, setup_w)
                    for dti in range(NDT):
                        P.op("pe", lambda E, d=dti: E.matmul(
                            p_proj[:], sxproj[:, d * 160:d * 160 + 128],
                            xcv[:, d * TC:(d + 1) * TC],
                            start=(d == 0), stop=(d == NDT - 1)),
                            incs=[(sPE, 1)] if dti == NDT - 1 else ())
                    for dti in range(NDT):
                        P.op("pe", lambda E, d=dti: E.matmul(
                            p_bc[0:16, :], sxproj[:, d * 160 + 128:d * 160 + 144],
                            xcv[:, d * TC:(d + 1) * TC],
                            start=(d == 0), stop=(d == NDT - 1)),
                            incs=[(sPE, 1)] if dti == NDT - 1 else ())
                    if not rc:
                        for dti in range(NDT):
                            P.op("pe", lambda E, d=dti: E.matmul(
                                p_bc[32:48, :],
                                sxproj[:, d * 160 + 144:d * 160 + 160],
                                xcv[:, d * TC:(d + 1) * TC],
                                start=(d == 0), stop=(d == NDT - 1)),
                                incs=[(sPE, 1)] if dti == NDT - 1 else ())
                    P.wait("act", sPE)
                    P.op("act", lambda E: E.copy(sdtin[:], p_proj[:]))
                    P.op("act", lambda E: E.copy(sbc[0:16, :], p_bc[0:16, :]))
                    if not rc:
                        P.op("act", lambda E: E.copy(sbc[32:48, :], p_bc[32:48, :]))
                    bc_done = sACT.v
                    P.wait("pe", sACT, bc_done)
                    sel = [(n, 0, Bb) for n in range(NST)]
                    if not rc:
                        sel = [x for n in range(NST)
                               for x in ((n, 0, Bb), (n, 32, Cb))]
                    copy_dones = []
                    for i, (n, row, dstT) in enumerate(sel):
                        pb = p_bb0 if i % 2 == 0 else p_bb1
                        if i >= 2:
                            P.wait("pe", sDVE, copy_dones[i - 2])
                        P.op("pe", lambda E, nn=n, r=row, pp=pb: E.matmul(
                            pp[:], seye[r:r + 16, nn * 128:(nn + 1) * 128],
                            sbc[r:r + 16, :],
                            start=True, stop=True), incs=[(sPE, 1)])
                        P.wait("dve", sPE)
                        P.op("dve", lambda E, nn=n, pp=pb, dd=dstT: E.tensor_copy(
                            dd[:, nn * TC:(nn + 1) * TC], pp[:]))
                        copy_dones.append(sDVE.v)
                    bc_copy_done[0] = sDVE.v
                    sp_done = [0] * (NDT + 1)
                    dl_done = [0] * (NDT + 1)
                    delta_done = [0] * (NDT + 1)
                    sdx_done = [0] * (NDT + 1)
                    mul_done = [[0, 0] for _ in range(NDT + 1)]
                    exps_done = [[0, 0] for _ in range(NDT)]
                    scan_done = [[0, 0] for _ in range(NDT)]
                    xo_done = [0] * NDT
                    sq_done = [0] * NDT
                    y0_done = [0] * NDT

                    def emit_mm(d):
                        pdl = p_dl0 if d % 2 == 0 else p_dl1
                        if d >= 2:
                            P.wait("pe", sACT, sp_done[d - 2])
                        if d == 0:
                            P.wait("pe", sACT, bc_done)
                        P.op("pe", lambda E, dd=d, pp=pdl: E.matmul(
                            pp[:], sdtproj[:, dd * 128:(dd + 1) * 128], sdtin[:],
                            start=True, stop=True), incs=[(sPE, 1)])
                        dl_done[d] = sPE.v

                    def emit_softplus(d):
                        pdl = p_dl0 if d % 2 == 0 else p_dl1
                        s0 = (d % 2) * 2 * TC
                        P.wait("act", sPE, dl_done[d])
                        if d >= 2:
                            P.wait("act", sDVE, delta_done[d - 2])
                        P.op("act", lambda E, dd=d, pp=pdl, c=s0: E.activation(
                            ssp[:, c:c + TC], pp[:], AF.Abs,
                            bias=sdtpb[:, dd:dd + 1]))
                        P.fence("act")
                        P.op("act", lambda E, c=s0: E.activation(
                            ssp[:, c:c + TC], ssp[:, c:c + TC], AF.Exp,
                            scale=-1.0))
                        P.fence("act")
                        P.op("act", lambda E, c=s0: E.activation(
                            ssp[:, c + TC:c + 2 * TC], ssp[:, c:c + TC], AF.Ln,
                            bias=1.0))
                        P.fence("act")
                        P.op("act", lambda E, dd=d, pp=pdl, c=s0: E.activation(
                            ssp[:, c:c + TC], pp[:], AF.Relu,
                            bias=sdtpb[:, dd:dd + 1]))
                        sp_done[d] = sACT.v

                    def emit_delta(d):
                        s0 = (d % 2) * 2 * TC
                        dl = (d % 2) * TC
                        P.wait("dve", sACT, sp_done[d])
                        if d >= 2:
                            P.wait("dve", sPOOL, sdx_done[d - 2])
                            P.wait("dve", sACT, exps_done[d - 2][1])
                        P.op("dve", lambda E, c=s0, dl_=dl: E.tensor_add(
                            sdelta[:, dl_:dl_ + TC], ssp[:, c:c + TC],
                            ssp[:, c + TC:c + 2 * TC]))
                        delta_done[d] = sDVE.v

                    def emit_exps(d, h):
                        dl = (d % 2) * TC
                        P.wait("act", sDVE, delta_done[d])
                        P.wait("act", ta_free[h][0], ta_free[h][1])
                        for n in range(NH):
                            g = d * NST + h * NH + n
                            P.op("act", lambda E, nn=n, gg=g, hh=h, dl_=dl:
                                 E.activation(
                                     tap[hh][:, nn * TC:(nn + 1) * TC],
                                     sdelta[:, dl_:dl_ + TC],
                                     AF.Exp, scale=sA[:, gg:gg + 1]))
                        exps_done[d][h] = sACT.v

                    def emit_muls(d, h):
                        dl = (d % 2) * TC
                        if h == 0:
                            P.fence("pool")
                            P.wait("pool", sDVE, delta_done[d])
                            P.wait("pool", sACT, xcv_done)
                            P.op("pool", lambda E, dd=d, dl_=dl: E.tensor_tensor(
                                out=sdx[:], in0=sdelta[:, dl_:dl_ + TC],
                                in1=xcv[:, dd * TC:(dd + 1) * TC], op=ALU.mult))
                            sdx_done[d] = sPOOL.v
                        P.fence("pool")
                        P.wait("pool", tb_free[h][0], tb_free[h][1])
                        P.wait("pool", sDVE, bc_copy_done[0])
                        for n in range(NH):
                            g = h * NH + n
                            P.op("pool", lambda E, nn=n, gg=g, hh=h:
                                 E.tensor_tensor(
                                     out=tbp[hh][:, nn * TC:(nn + 1) * TC],
                                     in0=sdx[:],
                                     in1=Bb[:, gg * TC:(gg + 1) * TC],
                                     op=ALU.mult))
                        mul_done[d][h] = sPOOL.v

                    def emit_scan(d, h):
                        cb = d * NST + h * NH
                        P.wait("dve", sACT, exps_done[d][h])
                        P.wait("dve", sPOOL, mul_done[d][h])
                        P.fence("dve")
                        P.op("dve", lambda E, hh=h, c=cb: E.tensor_tensor(
                            out=sinj[:], in0=ta3[hh][:, :, 0],
                            in1=carry[:, c:c + NH], op=ALU.mult))
                        P.fence("dve")
                        P.op("dve", lambda E, hh=h: E.tensor_tensor(
                            out=tb3[hh][:, :, 0], in0=tb3[hh][:, :, 0],
                            in1=sinj[:], op=ALU.add))
                        P.op("dve", lambda E, hh=h: E.tensor_scalar_mul(
                            ta3[hh][:, :, 0], ta3[hh][:, :, 0], 0.0))
                        P.fence("dve")
                        P.op("dve", lambda E, hh=h: E.tensor_tensor_scan(
                            out=tbp[hh][:], data0=tap[hh][:], data1=tbp[hh][:],
                            initial=scv[:, C_ZERO:C_ZERO + 1],
                            op0=ALU.mult, op1=ALU.add))
                        scan_done[d][h] = sDVE.v
                        P.fence("dve")
                        P.op("dve", lambda E, hh=h, c=cb: E.tensor_copy(
                            carry[:, c:c + NH], tb3[hh][:, :, TC - 1]))

                    def emit_pool_y(d):
                        P.fence("pool")
                        P.wait("pool", sDVE, scan_done[d][0])
                        P.op("pool", lambda E: E.tensor_tensor(
                            out=tap[0][:], in0=tbp[0][:],
                            in1=Cb[:, 0:NH * TC], op=ALU.mult))
                        tb_free[0] = (sPOOL, sPOOL.v)
                        y0_done[d] = sPOOL.v

                    def emit_xo(d):
                        o = d * XOS + PADW + base_t
                        P.fence("dve")
                        P.wait("dve", sPOOL, y0_done[d])
                        for w in (4 * TC, 2 * TC, TC):
                            P.op("dve", lambda E, ww=w: E.tensor_add(
                                tap[0][:, 0:ww], tap[0][:, 0:ww],
                                tap[0][:, ww:2 * ww]))
                            P.fence("dve")
                        P.op("dve", lambda E: E.tensor_copy(
                            y0[:], tap[0][:, 0:TC]))
                        ta_free[0] = (sDVE, sDVE.v)
                        P.op("dve", lambda E: E.tensor_mul(
                            tap[1][:], tbp[1][:], Cb[:, NH * TC:NST * TC]))
                        tb_free[1] = (sDVE, sDVE.v)
                        for w in (4 * TC, 2 * TC, TC):
                            P.fence("dve")
                            P.op("dve", lambda E, ww=w: E.tensor_add(
                                tap[1][:, 0:ww], tap[1][:, 0:ww],
                                tap[1][:, ww:2 * ww]))
                        if k < NCH - 1:
                            P.fence("dve")
                            P.op("dve", lambda E, oo=o, dd=d: E.tensor_copy(
                                hbuf[:, dd * 3:dd * 3 + 3],
                                xo[:, oo + TC - PADW:oo + TC]))
                        P.fence("dve")
                        P.op("dve", lambda E, oo=o, dd=d: E.scalar_tensor_tensor(
                            xo[:, oo:oo + TC], xo[:, oo:oo + TC],
                            sd1p[:, dd:dd + 1], tap[1][:, 0:TC],
                            ALU.mult, ALU.add))
                        ta_free[1] = (sDVE, sDVE.v)
                        P.fence("dve")
                        P.op("dve", lambda E, oo=o: E.tensor_add(
                            xo[:, oo:oo + TC], xo[:, oo:oo + TC], y0[:]))
                        xo_done[d] = sDVE.v

                    def emit_sq_stats(d):
                        o = d * XOS + PADW + base_t
                        P.wait("act", sDVE, xo_done[d])
                        P.wait("act", sPE, srb_free[0])
                        P.op("act", lambda E, oo=o: E.activation(
                            srq[:], xo[:, oo:oo + TC], AF.Square))
                        sq_done[d] = sACT.v
                        P.wait("pe", sDVE, xo_done[d])
                        P.wait("pe", sACT, sq_done[d])
                        P.op("pe", lambda E, dd=d, oo=o: E.matmul(
                            p_st[0:1, 0:TC], scv[:, C_ONE:C_ONE + 1],
                            xo[:, oo:oo + TC],
                            start=(dd == 0), stop=(dd == NDT - 1)))
                        P.op("pe", lambda E, dd=d: E.matmul(
                            p_st[0:1, TC:2 * TC], scv[:, C_ONE:C_ONE + 1],
                            srq[:], start=False, stop=(dd == NDT - 1)),
                            incs=[(sPE, 1)])
                        srb_free[0] = sPE.v

                    emit_mm(0)
                    emit_softplus(0)
                    emit_delta(0)
                    emit_muls(0, 0)
                    emit_muls(0, 1)
                    if NDT > 1:
                        emit_mm(1)
                    for d in range(NDT):
                        emit_exps(d, 0)
                        if not rc and d >= 1:
                            emit_sq_stats(d - 1)
                        if d + 1 < NDT:
                            emit_softplus(d + 1)
                        emit_exps(d, 1)
                        if d + 1 < NDT:
                            emit_delta(d + 1)
                        emit_scan(d, 0)
                        if rc:
                            ta_free[0] = (sDVE, scan_done[d][0])
                            tb_free[0] = (sDVE, sDVE.v)
                        else:
                            emit_pool_y(d)
                        if d + 1 < NDT:
                            emit_muls(d + 1, 0)
                        if d + 2 < NDT:
                            emit_mm(d + 2)
                        emit_scan(d, 1)
                        if rc:
                            ta_free[1] = (sDVE, scan_done[d][1])
                            tb_free[1] = (sDVE, sDVE.v)
                        else:
                            emit_xo(d)
                        if d + 1 < NDT:
                            emit_muls(d + 1, 1)
                    if not rc:
                        emit_sq_stats(NDT - 1)
                    if not rc:
                        P.wait("dve", sPE)
                        P.op("dve", lambda E: E.tensor_scalar_mul(
                            sst[0:1, 0:TC], p_st[0:1, 0:TC], 1.0 / D))
                        P.op("dve", lambda E: E.tensor_scalar_mul(
                            sst[0:1, TC:2 * TC], p_st[0:1, TC:2 * TC], 1.0 / D))
                        P.fence("dve")
                        P.op("dve", lambda E: E.tensor_mul(
                            sst[0:1, 2 * TC:3 * TC], sst[0:1, 0:TC],
                            sst[0:1, 0:TC]))
                        P.fence("dve")
                        P.op("dve", lambda E: E.tensor_sub(
                            sst[0:1, TC:2 * TC], sst[0:1, TC:2 * TC],
                            sst[0:1, 2 * TC:3 * TC]))
                        P.wait("act", sDVE)
                        P.op("act", lambda E: E.activation(
                            sst[0:1, 2 * TC:3 * TC], sst[0:1, TC:2 * TC], AF.Ln,
                            bias=scv[0:1, C_EPS:C_EPS + 1]))
                        P.fence("act")
                        P.op("act", lambda E: E.activation(
                            sst[0:1, TC:2 * TC], sst[0:1, 2 * TC:3 * TC], AF.Exp,
                            scale=scv[0:1, C_NEGHALF:C_NEGHALF + 1]))
                        P.wait("pe", sACT)
                        P.op("pe", lambda E: E.matmul(
                            p_bcst[:, 0:TC], sones1[:], sst[0:1, 0:TC],
                            start=True, stop=True))
                        P.op("pe", lambda E: E.matmul(
                            p_bcst[:, TC:2 * TC], sones1[:], sst[0:1, TC:2 * TC],
                            start=True, stop=True), incs=[(sPE, 1)])
                        P.wait("dve", sPE)
                        for dti in range(NDT):
                            xoff = dti * XOS + PADW + base_t
                            P.op("dve", lambda E, o=xoff: E.tensor_sub(
                                xo[:, o:o + TC], xo[:, o:o + TC], p_bcst[:, 0:TC]))
                            P.fence("dve")
                            P.op("dve", lambda E, o=xoff: E.tensor_mul(
                                xo[:, o:o + TC], xo[:, o:o + TC],
                                p_bcst[:, TC:2 * TC]))
                            mul_done = sDVE.v
                            P.wait("pool", sDVE, mul_done)
                            P.op("pool", lambda E, o=xoff, d=dti: E.tensor_copy(
                                xb[:, d * MC + base_t:d * MC + base_t + TC],
                                xo[:, o:o + TC]))
                    barc["pe"] = sPE.v
                    barc["act"] = sACT.v
                    barc["dve"] = sDVE.v
                    barc["pool"] = sPOOL.v

                for g in range(4):
                    P.op("sp", lambda E, gg=g: E.dma_start(
                        out=wbufv[:, 0:NDT * WTS].rearrange(
                            "p (dt t) -> p dt t", dt=NDT)
                        [:, gg * 4:(gg + 1) * 4, :],
                        in_=xdt_pre.rearrange("(dt p) t -> p dt t", p=128)
                        [:, gg * 4:(gg + 1) * 4, 0:WTS]),
                        incs=[(sRC, 16)])
                for kk in range(NCH):
                    mamba_chunk(True, kk)
                P.wait("dve", sIN, setup_in)
                P.fence("dve")
                P.op("dve", lambda E: E.tensor_scalar_mul(
                    carry[:], carry[:], srcm[:, 0:1]))
                P.fence("dve")
                for kk in range(NCH):
                    mamba_chunk(False, kk)

                if debug:
                    bwait("sp")
                    for dti in range(NDT):
                        P.op("sp", lambda E, d=dti: E.dma_start(
                            out=dbg_xin[:, d * MC:(d + 1) * MC],
                            in_=xo[:, d * XOS + PADW:d * XOS + PADW + MC]),
                            incs=[(sOUT, 16)])

            acc = sbuf("acc", [128, NTT * D], F32)
            xloq = sbuf("xloq", [128, 4 * 128], BF16)
            sbt = sbuf("sbt", [128, 512], BF16)
            with ExitStack() as tstack:
                p_lg = tstack.enter_context(nc.psum_tensor("p_lg", [128, 8], F32))
                p_tr = tstack.enter_context(
                    nc.psum_tensor("p_tr", [128, 1024], BF16))

                bwait("pe")
                bwait("act")
                bwait("dve")
                for tt in range(NTT):
                    for dti in range(NDT):
                        P.op("pe", lambda E, d=dti, t=tt: E.matmul(
                            p_lg[:],
                            xo[:, d * XOS + PADW + t * 128:
                               d * XOS + PADW + (t + 1) * 128],
                            swg[:, d * 8:(d + 1) * 8],
                            start=(d == 0), stop=(d == NDT - 1)),
                            incs=[(sPE, 1)] if dti == NDT - 1 else ())

                    def r0(c):
                        return lsc[:, c * 8:(c + 1) * 8]
                    P.wait("dve", sPE)
                    P.op("dve", lambda E: E.reduce_max(lsc[:, 64:65], p_lg[:], AX.X))
                    P.fence("dve")
                    P.op("dve", lambda E: E.tensor_scalar_mul(
                        lsc[:, 65:66], lsc[:, 64:65], -1.0))
                    P.op("dve", lambda E: E.tensor_copy(r0(0), p_lg[:]))
                    P.wait("act", sDVE)
                    P.op("act", lambda E: E.activation(
                        r0(1), r0(0), AF.Exp, bias=lsc[:, 65:66]))
                    P.wait("dve", sACT)
                    P.op("dve", lambda E: E.reduce_max(lsc[:, 64:65], r0(1), AX.X))
                    P.fence("dve")
                    P.op("dve", lambda E: E.tensor_single_scalar(
                        r0(2), r0(1), lsc[:, 64:65], ALU.is_equal))
                    P.fence("dve")
                    P.op("dve", lambda E: E.tensor_scalar(
                        r0(3), r0(2), -1.0, 1.0, ALU.mult, ALU.add))
                    P.fence("dve")
                    P.op("dve", lambda E: E.tensor_mul(r0(3), r0(3), r0(1)))
                    P.fence("dve")
                    P.op("dve", lambda E: E.reduce_max(lsc[:, 66:67], r0(3), AX.X))
                    P.fence("dve")
                    P.op("dve", lambda E: E.tensor_single_scalar(
                        r0(4), r0(3), lsc[:, 66:67], ALU.is_equal))
                    P.fence("dve")
                    P.op("dve", lambda E: E.tensor_add(r0(2), r0(2), r0(4)))
                    P.op("dve", lambda E: E.tensor_add(
                        lsc[:, 67:68], lsc[:, 64:65], lsc[:, 66:67]))
                    P.fence("dve")
                    P.op("dve", lambda E: E.reciprocal(lsc[:, 68:69], lsc[:, 67:68]))
                    P.fence("dve")
                    P.op("dve", lambda E: E.tensor_mul(r0(2), r0(2), r0(1)))
                    P.fence("dve")
                    P.op("dve", lambda E, t=tt: E.tensor_scalar_mul(
                        comb[:, t * 8:(t + 1) * 8], r0(2), lsc[:, 68:69]))
                    P.wait("pe", sDVE)

                if debug:
                    P.wait("sp", sDVE)
                    P.op("sp", lambda E: E.dma_start(out=dbg_comb, in_=comb[:]),
                         incs=[(sOUT, 16)])

                for tt in range(NTT):
                    for dq in range(4):
                        P.wait("dve", sPE)
                        for i in range(4):
                            dti = dq * 4 + i
                            xoff = dti * XOS + PADW + tt * 128
                            P.op("dve", lambda E, o=xoff, d=dti, ii=i, t=tt:
                                 E.tensor_sub(
                                     xloq[:, ii * 128:(ii + 1) * 128],
                                     xo[:, o:o + 128],
                                     xb[:, d * MC + t * 128:d * MC + t * 128 + 128]))
                        P.wait("pe", sDVE)
                        for i in range(4):
                            dti = dq * 4 + i
                            P.op("pe", lambda E, d=dti, ii=i, t=tt: E.transpose(
                                p_tr[:, ii * 128:(ii + 1) * 128],
                                xb[:, d * MC + t * 128:d * MC + t * 128 + 128],
                                sident[:]))
                            P.op("pe", lambda E, ii=i: E.transpose(
                                p_tr[:, 512 + ii * 128:512 + (ii + 1) * 128],
                                xloq[:, ii * 128:(ii + 1) * 128], sident[:]),
                                incs=[(sPE, 1)] if i == 3 else ())
                        P.wait("dve", sPE)
                        P.op("dve", lambda E: E.tensor_copy(sbt[:], p_tr[:, 0:512]))
                        P.fence("dve")
                        P.op("dve", lambda E, t=tt, q=dq: E.tensor_add(
                            acc[:, t * D + q * 512:t * D + (q + 1) * 512],
                            sbt[:], p_tr[:, 512:1024]))
                seed_dve = sDVE.v

        H = sbuf("H", [128, NF * MC], BF16, side="right")
        wgb = [sbuf("wgb0", [128, D], BF16), sbuf("wgb1", [128, D], BF16)]
        wub = [sbuf("wub0", [128, D], BF16), sbuf("wub1", [128, D], BF16)]
        wdb = [sbuf("wdb0", [128, D], BF16), sbuf("wdb1", [128, D], BF16)]
        tsil = sbuf("tsil", [128, MC], F32)
        sq16 = sbuf("sq16", [128, D], BF16)
        pG = en(nc.psum_tensor("pG", [128, MC], F32))
        pU = en(nc.psum_tensor("pU", [128, MC], F32))
        pdA = en(nc.psum_tensor("pdA", [128, MC], F32))
        pdB = en(nc.psum_tensor("pdB", [128, MC], F32))

        def pd_slice(blk, tt):
            grpA = (blk % 2 == 0)
            if tt < 4:
                base = pG if grpA else pdA
            else:
                base = pU if grpA else pdB
            c = (tt % 4) * 256
            return base[:, c:c + 256]

        ngl = [0]
        ndl = [0]
        silu_cap = []
        hm_cap = []
        ev_cap = []

        for e in range(n_experts):
            for f in range(NF):
                fgi = e * NF + f
                par = f % 2
                if ngl[0] == 0:
                    P.wait("pool", sDVE, seed_dve)
                if ngl[0] >= 2:
                    P.wait("pool", sGf, ngl[0] - 1)
                    P.wait("pool", sUf, ngl[0] - 1)
                row = (e * NF + f) * 128
                P.op("pool", lambda E, r=row, p=par: E.dma_start(
                    out=wgb[p][:], in_=gate_w[r:r + 128, :]),
                    incs=[(sWg[par], 16)])
                P.op("pool", lambda E, r=row, p=par: E.dma_start(
                    out=wub[p][:], in_=up_w[r:r + 128, :]),
                    incs=[(sWu[par], 16)])
                ngl[0] += 1
                P.wait("pe", sWg[par], sWg[par].v)
                if fgi > 0:
                    P.wait("pe", sACT, silu_cap[fgi - 1])
                if e == 0 and f == 0:
                    P.wait("pe", sDVE, seed_dve)
                if f == 0 and e > 0:
                    P.wait("pe", sDVE, ev_cap[-1])
                for dti in range(NDT):
                    for h in range(2):
                        P.op("pe", lambda E, d=dti, hh=h, p=par: E.matmul(
                            pG[:, hh * 512:(hh + 1) * 512],
                            wgb[p][:, d * 128:(d + 1) * 128],
                            xb[:, d * MC + hh * 512:d * MC + (hh + 1) * 512],
                            start=(d == 0), stop=(d == NDT - 1)),
                            incs=[(sGf, 1)] if (dti == NDT - 1 and h == 1) else ())
                P.wait("pe", sWu[par], sWu[par].v)
                if fgi > 0:
                    P.wait("pe", sDVE, hm_cap[fgi - 1])
                for dti in range(NDT):
                    for h in range(2):
                        P.op("pe", lambda E, d=dti, hh=h, p=par: E.matmul(
                            pU[:, hh * 512:(hh + 1) * 512],
                            wub[p][:, d * 128:(d + 1) * 128],
                            xb[:, d * MC + hh * 512:d * MC + (hh + 1) * 512],
                            start=(d == 0), stop=(d == NDT - 1)),
                            incs=[(sUf, 1)] if (dti == NDT - 1 and h == 1) else ())
                P.wait("act", sGf, fgi + 1)
                if fgi > 0:
                    P.wait("act", sDVE, hm_cap[fgi - 1])
                P.op("act", lambda E: E.activation(tsil[:], pG[:], AF.Silu))
                silu_cap.append(sACT.v)
                P.wait("dve", sACT, silu_cap[fgi])
                P.wait("dve", sUf, sUf.v)
                P.op("dve", lambda E, ff=f: E.tensor_tensor(
                    H[:, ff * MC:(ff + 1) * MC], tsil[:], pU[:], ALU.mult))
                hm_cap.append(sDVE.v)
            P.wait("pe", sDVE, hm_cap[-1])
            for blk in range(8):
                for fq in range(4):
                    if ndl[0] >= 2:
                        P.wait("pool", sDf, ndl[0] - 1)
                    rowd = (e * NF + blk * 4 + fq) * 128
                    pard = ndl[0] % 2
                    P.op("pool", lambda E, r=rowd, p=pard: E.dma_start(
                        out=wdb[p][:], in_=down_w[r:r + 128, :]),
                        incs=[(sWd[pard], 16)])
                    ndl[0] += 1
                    P.wait("pe", sWd[pard], sWd[pard].v)
                    if fq == 0 and len(ev_cap) >= 2:
                        P.wait("pe", sDVE, ev_cap[-2])
                    for f8 in range(8):
                        f = fq * 8 + f8
                        for tt in range(NTT):
                            last_piece = (f8 == 7 and tt == NTT - 1)
                            P.op("pe", lambda E, ff=f, t=tt, b=blk, p=pard, q=fq,
                                 f8_=f8: E.matmul(
                                     pd_slice(b, t),
                                     H[:, ff * MC + t * 128:ff * MC + (t + 1) * 128],
                                     wdb[p][:, f8_ * 256:(f8_ + 1) * 256],
                                     start=(q == 0 and f8_ == 0 and t % 2 == 0),
                                     stop=(q == 3 and f8_ == 7)),
                                 incs=[(sDf, 1)] if last_piece else ())
                P.wait("dve", sDf, ndl[0])
                for tt in range(NTT):
                    P.fence("dve")
                    if e == 0:
                        P.op("dve", lambda E, b=blk, t=tt: E.scalar_tensor_tensor(
                            acc[:, t * D + b * 256:t * D + (b + 1) * 256],
                            pd_slice(b, t), 1.0,
                            acc[:, t * D + b * 256:t * D + (b + 1) * 256],
                            ALU.mult, ALU.add))
                    else:
                        P.op("dve", lambda E, b=blk, t=tt, ee=e:
                             E.scalar_tensor_tensor(
                                 acc[:, t * D + b * 256:t * D + (b + 1) * 256],
                                 pd_slice(b, t),
                                 comb[:, t * 8 + ee - 1:t * 8 + ee],
                                 acc[:, t * D + b * 256:t * D + (b + 1) * 256],
                                 ALU.mult, ALU.add))
                ev_cap.append(sDVE.v)

        P.wait("dve", sDVE, ev_cap[-1])
        for tt in range(NTT):
            a_tt = acc[:, tt * D:(tt + 1) * D]
            P.fence("dve")
            P.op("dve", lambda E, a=a_tt: E.reduce_sum(lsc[:, 64:65], a, AX.X))
            P.fence("dve")
            P.op("dve", lambda E: E.tensor_scalar_mul(
                lsc[:, 64:65], lsc[:, 64:65], 1.0 / D))
            P.fence("dve")
            P.op("dve", lambda E, a=a_tt: E.tensor_single_scalar(
                a, a, lsc[:, 64:65], ALU.subtract))
            P.wait("act", sDVE)
            P.op("act", lambda E, a=a_tt: E.activation(sq16[:], a, AF.Square))
            P.wait("dve", sACT)
            P.op("dve", lambda E: E.reduce_sum(lsc[:, 65:66], sq16[:], AX.X))
            P.fence("dve")
            P.op("dve", lambda E: E.tensor_scalar_mul(
                lsc[:, 65:66], lsc[:, 65:66], 1.0 / D))
            P.wait("act", sDVE)
            P.op("act", lambda E: E.activation(
                lsc[:, 66:67], lsc[:, 65:66], AF.Ln,
                bias=scv[:, C_EPS:C_EPS + 1]))
            P.fence("act")
            P.op("act", lambda E: E.activation(
                lsc[:, 67:68], lsc[:, 66:67], AF.Exp,
                scale=scv[:, C_NEGHALF:C_NEGHALF + 1]))
            P.wait("dve", sACT)
            P.op("dve", lambda E, a=a_tt: E.tensor_scalar_mul(
                a, a, lsc[:, 67:68]))
            P.wait("sp", sDVE)
            P.op("sp", lambda E, t=tt, a=a_tt: E.dma_start(
                out=out_hbm[t * 128:(t + 1) * 128, :], in_=a),
                incs=[(sOUT, 16)])
        fin = sOUT.v
        P.q["sp"].append(lambda E, t=fin: E.wait_ge(sOUT.h, t))

        with nc.Block() as block:
            @block.tensor
            def _(E):
                for fn in P.q["pe"]:
                    fn(E)

            @block.scalar
            def _(E):
                for fn in P.q["act"]:
                    fn(E)

            @block.vector
            def _(E):
                for fn in P.q["dve"]:
                    fn(E)

            @block.gpsimd
            def _(E):
                for fn in P.q["pool"]:
                    fn(E)

            @block.sync
            def _(E):
                for fn in P.q["sp"]:
                    fn(E)

    return nc



def _prep_shared(i):
    f32 = np.float32
    import ml_dtypes
    sh = {}
    sh["cvec"] = np.tile(np.array([[EPS, -0.5, 0.0, 1.0]], dtype=f32), (128, 1))
    cw = np.ascontiguousarray(np.asarray(i["L0_attn_conv_w"], f32)[:, 0, :])
    sh["convw_t"] = np.ascontiguousarray(
        cw.reshape(NDT, 128, 4).transpose(1, 0, 2).reshape(128, NDT * 4))
    for name, vec in [("convb_t", i["L0_attn_conv_b"]),
                      ("dtprojb_t", i["L0_attn_dtproj_b"]),
                      ("d1p_t", 1.0 + np.asarray(i["L0_attn_D_param"], f32))]:
        sh[name] = np.ascontiguousarray(
            np.asarray(vec, dtype=f32).reshape(NDT, 128).T)
    sh["alog_t"] = np.ascontiguousarray(
        np.asarray(i["L0_attn_A_log"], f32).reshape(NDT, 128, NST)
        .transpose(1, 0, 2).reshape(128, NDT * NST))
    xp = np.asarray(i["L0_attn_xproj_w"], f32)
    sh["xproj_t"] = np.ascontiguousarray(
        xp.T.reshape(NDT, 128, 160).transpose(1, 0, 2).reshape(128, NDT * 160))
    sh["dtproj_t"] = np.ascontiguousarray(np.asarray(i["L0_attn_dtproj_w"], f32).T)
    wg = np.asarray(i["L0_ffn_wg"], f32)
    sh["wg_t"] = np.ascontiguousarray(
        wg.T.reshape(NDT, 128, 8).transpose(1, 0, 2).reshape(128, NDT * 8))
    eye = np.zeros((48, NST * 128), dtype=f32)
    for n in range(NST):
        eye[n, n * 128:(n + 1) * 128] = 1.0
        eye[32 + n, n * 128:(n + 1) * 128] = 1.0
    sh["eye16"] = eye.astype(ml_dtypes.bfloat16)
    sh["ones1_128"] = np.ones((1, 128), dtype=f32)
    sh["ones128_bf"] = np.ones((128, 1), dtype=ml_dtypes.bfloat16)
    sh["ident_bf"] = np.eye(128, dtype=f32).astype(ml_dtypes.bfloat16)

    def tile_gu(mats):
        out = np.empty((NE * NF * 128, D), dtype=f32)
        for e, W in enumerate(mats):
            t = W.reshape(NF, 128, NDT, 128).transpose(0, 3, 2, 1)
            out[e * NF * 128:(e + 1) * NF * 128] = t.reshape(NF * 128, D)
        return out

    gmats = [np.asarray(i["L0_ffn_sh_gate"], f32)] + \
            [np.asarray(i["L0_ffn_ex_gate"][e], f32) for e in range(8)]
    umats = [np.asarray(i["L0_ffn_sh_up"], f32)] + \
            [np.asarray(i["L0_ffn_ex_up"][e], f32) for e in range(8)]
    sh["gate_w"] = tile_gu(gmats)
    sh["up_w"] = tile_gu(umats)

    def tile_down(mats):
        out = np.empty((NE * NF * 128, D), dtype=f32)
        for e, Wd in enumerate(mats):
            t = Wd.reshape(8, 256, 4, 8, 128)
            t = t.transpose(0, 2, 4, 3, 1)
            out[e * NF * 128:(e + 1) * NF * 128] = t.reshape(NF * 128, D)
        return out

    dmats = [np.asarray(i["L0_ffn_sh_down"], f32)] + \
            [np.asarray(i["L0_ffn_ex_down"][e], f32) for e in range(8)]
    sh["down_w"] = tile_down(dmats)
    return sh


def _prep_core(x, b, h):
    f32 = np.float32
    xt = np.asarray(x[b], f32).T
    own = np.zeros((D, XOS), dtype=f32)
    own[:, PADW:] = xt[:, h * MC:(h + 1) * MC]
    if h == 1:
        own[:, 0:PADW] = xt[:, MC - PADW:MC]
    pre = np.zeros((D, XOS), dtype=f32)
    if h == 1:
        pre[:, PADW:] = xt[:, 0:MC]
    return np.ascontiguousarray(own), np.ascontiguousarray(pre)


def kernel(**inputs):
    x = np.asarray(inputs["L0_attn_x"], np.float32)
    sh = _prep_shared(inputs)
    nc = build_program(debug=False)
    in_maps = []
    for core in range(8):
        b, h = core // 2, core % 2
        own, pre = _prep_core(x, b, h)
        m = dict(sh)
        m["xdt_own"] = own
        m["xdt_pre"] = pre
        m["rcmask"] = np.full((128, 1), 1.0 if h == 1 else 0.0, np.float32)
        in_maps.append(m)
    res = run_bass_kernel_spmd(nc, in_maps, core_ids=list(range(8)))
    out = np.empty((4, 2048, D), dtype=np.float32)
    for core in range(8):
        b, h = core // 2, core % 2
        out[b, h * MC:(h + 1) * MC, :] = res.results[core]["out"]
    return out
